# version 04
# baseline (speedup 1.0000x reference)
import contextlib
import numpy as np
import ml_dtypes
import concourse.bass as bass
import concourse.mybir as mybir
from concourse.bass_utils import run_bass_kernel_spmd

F32 = mybir.dt.float32
BF16 = mybir.dt.bfloat16
AF = mybir.ActivationFunctionType
ALU = mybir.AluOpType
AX = mybir.AxisListType
FMAGIC = 12582912.0
T = 2048
C = 1024
NT = T // 128
NC = C // 128
LN_EPS = 1e-5


class W:

    def __init__(self, p, eng, name, flush=True, dma_inc=None):
        self.p = p
        self.e = eng
        self.name = name
        self.sem = p.sem("ch_" + name)
        self.n = 0
        self.flush = flush
        self.dma_inc = dma_inc
        self._waited = {}

    def wait(self, handle):
        sem, val = handle
        if sem is self.sem:
            pass
        key = id(sem)
        if self._waited.get(key, -1) >= val:
            return
        self.e.wait_ge(sem, val)
        self._waited[key] = val

    def waits(self, *handles):
        for h in handles:
            if h is not None:
                self.wait(h)

    def milestone(self):
        return (self.sem, self.n)

    def _op(self, fn, a, k):
        if self.flush and self.n > 0:
            self.wait((self.sem, self.n))
        inst = fn(*a, **k)
        inc = self.dma_inc or 1
        inst.then_inc(self.sem, inc)
        self.n += inc
        return (self.sem, self.n)

    def __getattr__(self, name):
        fn = getattr(self.e, name)

        def call(*a, **k):
            return self._op(fn, a, k)
        return call

    def dma(self, out, in_, **k):
        inst = self.e.dma_start(out=out, in_=in_, **k)
        inst.then_inc(self.sem, 16)
        self.n += 16
        return (self.sem, self.n)


class E:

    def __init__(self, p):
        self.v = W(p, p.nc.vector, "v")
        self.s = W(p, p.nc.scalar, "s")
        self.g = W(p, p.nc.gpsimd, "g")
        self.t = W(p, p.nc.tensor, "t", flush=False)
        self.q = W(p, p.nc.sync, "q")
        self.gq = W(p, p.nc.gpsimd, "gq")
        self.sq = W(p, p.nc.scalar, "sq")


class Prog:
    def __init__(self, name="prog"):
        self.nc = bass.Bass("TRN2", target_bir_lowering=False, debug=False)
        self.ctx = contextlib.ExitStack()
        self._uid = 0
        self._consts = {}
        self._ctile = None
        self._need_ident = False
        self._need_ones_f32 = False
        self._need_ones_bf = False

    def uid(self):
        self._uid += 1
        return self._uid

    def sb(self, name, shape, dt=F32):
        stack = self._scopes[-1] if getattr(self, "_scopes", None) else self.ctx
        return stack.enter_context(self.nc.sbuf_tensor(f"{name}_{self.uid()}", shape, dt))

    def ps(self, name, shape, dt=F32):
        stack = self._scopes[-1] if getattr(self, "_scopes", None) else self.ctx
        return stack.enter_context(self.nc.psum_tensor(f"{name}_{self.uid()}", shape, dt))

    def push_scope(self):
        if not hasattr(self, "_scopes"):
            self._scopes = []
        self._scopes.append(contextlib.ExitStack())

    def pop_scope(self):
        self._scopes.pop().close()

    def sem(self, name):
        h = self.ctx.enter_context(self.nc.semaphore(f"{name}_{self.uid()}"))
        if not hasattr(self, "_my_sems"):
            self._my_sems = []
        self._my_sems.append(h)
        return h

    def dram_in(self, name, shape, dt=F32):
        return self.nc.dram_tensor(name, shape, dt, kind="ExternalInput").ap()

    def dram_out(self, name, shape, dt=F32):
        return self.nc.dram_tensor(name, shape, dt, kind="ExternalOutput").ap()

    def dram_scratch(self, name, shape, dt=F32):
        return self.nc.dram_tensor(f"{name}_{self.uid()}", shape, dt, kind="Internal").ap()

    def cval(self, v):
        if self._ctile is None:
            self._ctile = self.sb("constpool", [128, 16], F32)
        if v not in self._consts:
            assert not getattr(self, "_consts_emitted", False), \
                f"const {v} registered after emit_consts -- add it to the emitter's CONSTS"
            assert len(self._consts) < 16
            self._consts[v] = len(self._consts)
        return self._ctile[:, self._consts[v]:self._consts[v] + 1]

    def prep_consts(self, *vals, ident=False, ones_f32=False, ones_bf=False):
        for v in vals:
            self.cval(v)
        if ident:
            self._need_ident = True
            self.identb = self.sb("identb", [128, 128], BF16)
        if ones_f32:
            self._need_ones_f32 = True
            self.onesf = self.sb("onesf", [1, 128], F32)
        if ones_bf:
            self._need_ones_bf = True
            self.onesb = self.sb("onesb", [1, 128], BF16)

    def emit_consts(self, e):
        self._consts_emitted = True
        for v, i in self._consts.items():
            e.g.memset(self._ctile[:, i:i + 1], float(v))
        if self._need_ident:
            e.g.memset(self.identb[:, :], 1.0)
            e.g.affine_select(self.identb[:, :], self.identb[:, :], [[-1, 128]],
                              ALU.is_equal, 0.0, base=0, channel_multiplier=1)
        if self._need_ones_f32:
            e.g.memset(self.onesf[:, :], 1.0)
        if self._need_ones_bf:
            e.g.memset(self.onesb[:, :], 1.0)
        return e.g.milestone()

    def run(self, body):
        from concourse.bass import get_kernel_semaphore_range
        with self.nc.Block() as blk:
            @blk.sync
            def _(sync):
                e = E(self)
                body(e)
                g = self.nc.gpsimd
                for w in (e.v, e.s, e.t, e.g, e.q, e.gq, e.sq):
                    if w.n > 0 and w.sem is not self.nc.gpsimd:
                        g.wait_ge(w.sem, w.n)
                nums = sorted(s.num for s in self._my_sems)
                lo, hi = nums[0], nums[-1] + 1
                assert nums == list(range(lo, hi)), nums
                g.dma_reset(range(lo, hi))
                g.sem_clear(range(lo, hi))
        self.ctx.close()
        return self.nc


class QuantKit:
    def __init__(self, p, width=1024):
        self.p = p
        self.width = width
        nb = width // 32
        self.amax = p.sb("q_amax", [128, nb], F32)
        self.rs = p.sb("q_rs", [128, nb], F32)
        self.t = p.sb("q_t", [128, width], F32)
        self.a = p.sb("q_a", [128, width], F32)
        self.m1 = p.sb("q_m1", [128, width], F32)
        self.m2 = p.sb("q_m2", [128, width], F32)
        self.v1 = p.sb("q_v1", [128, width], F32)

    def emit(self, v, src_f32, dst_bf, scale_sl, width=None):
        w = width or self.width
        nb = w // 32
        r32 = lambda ap: ap.rearrange("p (b j) -> p b j", j=32)
        sl = lambda tile: tile[:, 0:w]
        v.reduce_max(self.amax[:, 0:nb].unsqueeze(2), r32(src_f32),
                     axis=AX.X, apply_absolute_value=True)
        v.tensor_scalar(scale_sl, self.amax[:, 0:nb], 1.0 / 7.5, 1e-30, ALU.mult, ALU.max)
        v.reciprocal(self.rs[:, 0:nb], scale_sl)
        rsrep = self.rs[:, 0:nb].unsqueeze(2).broadcast_to([128, nb, 32])
        screp = scale_sl.unsqueeze(2).broadcast_to([128, nb, 32])
        v.tensor_tensor(r32(sl(self.t)), r32(src_f32), rsrep, ALU.mult)
        v.tensor_tensor(sl(self.a), sl(self.t), sl(self.t), ALU.mult)
        v.tensor_scalar(sl(self.m1), sl(self.a), 4.0, None, ALU.is_ge)
        v.tensor_scalar(sl(self.m2), sl(self.a), 16.0, None, ALU.is_ge)
        v.tensor_scalar(sl(self.a), sl(self.m2), -2.0, 8.0, ALU.mult, ALU.add)
        v.scalar_tensor_tensor(sl(self.v1), sl(self.m1), -4.0, sl(self.a),
                               ALU.mult, ALU.add)
        v.tensor_tensor(sl(self.a), sl(self.t), sl(self.v1), ALU.mult)
        v.tensor_scalar(sl(self.v1), sl(self.a), FMAGIC, None, ALU.add)
        v.tensor_scalar(sl(self.a), sl(self.v1), FMAGIC, None, ALU.subtract)
        v.tensor_scalar(sl(self.v1), sl(self.m2), 0.25, 0.125, ALU.mult, ALU.add)
        v.scalar_tensor_tensor(sl(self.m2), sl(self.m1), 0.125, sl(self.v1),
                               ALU.mult, ALU.add)
        v.tensor_tensor(r32(sl(self.m1)), r32(sl(self.m2)), screp, ALU.mult)
        with self.p.nc.allow_low_precision(reason="bf16 matmul operand"):
            out = v.tensor_tensor(dst_bf, sl(self.a), sl(self.m1), ALU.mult)
        return out


class LNKit:
    def __init__(self, p):
        self.p = p
        self.bn12 = p.sb("ln_bn12", [128, 12], F32)
        self.mv = p.sb("ln_mv", [128, 2], F32)
        self.rstd = p.sb("ln_rstd", [128, 1], F32)

    def emit(self, e, s2, dst):
        v, s = e.v, e.s
        v.bn_stats(self.bn12[:, 0:6], s2[:, 0:512])
        v.bn_stats(self.bn12[:, 6:12], s2[:, 512:1024])
        h_mv = v.bn_aggr(self.mv[:, :], self.bn12[:, :])
        s.wait(h_mv)
        s.activation(self.rstd[:, :], self.mv[:, 1:2], AF.Ln,
                     bias=self.p.cval(LN_EPS), scale=1.0)
        h_r = s.activation(self.rstd[:, :], self.rstd[:, :], AF.Exp, bias=0.0,
                           scale=self.p.cval(-0.5))
        v.wait(h_r)
        return v.tensor_scalar(dst, s2, self.mv[:, 0:1], self.rstd[:, :],
                               ALU.subtract, ALU.mult)


def emit_l1ffn(p, e, io):
    nc = p.nc
    xres = io["xres"]
    h_x = io["x_ready"]

    xd = p.sb("xd", [128, NT * 1024], BF16)
    xdT = p.sb("xdT", [128, NC * T], BF16)
    wnat = p.sb("wnat", [128, 1024], F32)
    wd = p.sb("wd", [128, NC * 1024], BF16)
    wdT = p.sb("wdT", [128, NC * 1024], BF16)
    bias_row = p.sb("biasrow", [1, 1024], F32)
    biasrep = p.sb("biasrep", [128, 1024], F32)
    q_scale = p.sb("q_scale", [128, NT * 32], F32)
    s2 = p.sb("s2", [128, 1024], F32)
    qk = QuantKit(p)
    ln = LNKit(p)

    ps_rep = [p.ps(f"psrep{i}", [128, 512], F32) for i in range(2)]
    ps_tp = [p.ps(f"pstp{i}", [128, 128], BF16) for i in range(2)]
    ps_mm = [p.ps(f"psmm{i}", [128, 512], F32) for i in range(2)]

    h_b = e.q.dma(bias_row[:, :], io["bias_h"][:, :])
    e.t.waits(h_b, io["consts"])
    h_repmm = [e.t.matmul(ps_rep[h][:, :], p.onesf[:, :],
                          bias_row[:, h * 512:(h + 1) * 512], start=True, stop=True)
               for h in range(2)]
    for h in range(2):
        e.s.wait(h_repmm[h])
        e.s.copy(biasrep[:, h * 512:(h + 1) * 512], ps_rep[h][:, :])

    e.v.wait(h_x)
    h_xq = []
    for tc in range(NT):
        h_xq.append(qk.emit(e.v, xres[:, tc * 1024:(tc + 1) * 1024],
                            xd[:, tc * 1024:(tc + 1) * 1024],
                            q_scale[:, tc * 32:(tc + 1) * 32]))

    wt = io["weight_h"].rearrange("(n p) c -> n p c", p=128)
    h_wq = []
    for k in range(NC):
        if k >= 1:
            e.q.wait(h_wq[k - 1])
        h_wl = e.q.dma(wnat[:, :], wt[k])
        e.v.wait(h_wl)
        h_wq.append(qk.emit(e.v, wnat[:, :],
                            wd[:, k * 1024:(k + 1) * 1024],
                            q_scale[:, k * 32:(k + 1) * 32]))

    h_tp_ev = []
    ntp = 0
    for k in range(NC):
        e.t.wait(h_wq[k])
        for j in range(NC):
            if ntp >= 2:
                e.t.wait(h_tp_ev[ntp - 2])
            h = e.t.transpose(ps_tp[ntp % 2][:, :],
                              wd[:, k * 1024 + j * 128: k * 1024 + (j + 1) * 128],
                              p.identb[:, :])
            e.s.wait(h)
            h_tp_ev.append(e.s.copy(wdT[:, j * 1024 + k * 128: j * 1024 + (k + 1) * 128],
                                    ps_tp[ntp % 2][:, :]))
            ntp += 1
    for tc in range(NT):
        e.t.wait(h_xq[tc])
        for j in range(NC):
            if ntp >= 2:
                e.t.wait(h_tp_ev[ntp - 2])
            h = e.t.transpose(ps_tp[ntp % 2][:, :],
                              xd[:, tc * 1024 + j * 128: tc * 1024 + (j + 1) * 128],
                              p.identb[:, :])
            e.s.wait(h)
            h_tp_ev.append(e.s.copy(xdT[:, j * T + tc * 128: j * T + (tc + 1) * 128],
                                    ps_tp[ntp % 2][:, :]))
            ntp += 1
    h_alltp = h_tp_ev[-1]

    e.t.wait(h_alltp)
    h_ev = {}
    h_out = None
    for tc in range(NT):
        for hh in range(2):
            if tc >= 1:
                e.t.wait(h_ev[(tc - 1, hh)])
            for k in range(NC):
                h_mm = e.t.matmul(ps_mm[hh][:, :],
                                  xdT[:, k * T + tc * 128: k * T + (tc + 1) * 128],
                                  wdT[:, k * 1024 + hh * 512: k * 1024 + (hh + 1) * 512],
                                  start=(k == 0), stop=(k == NC - 1))
            e.v.wait(h_mm)
            h_ev[(tc, hh)] = e.v.tensor_tensor(
                s2[:, hh * 512:(hh + 1) * 512], ps_mm[hh][:, :],
                biasrep[:, hh * 512:(hh + 1) * 512], ALU.add)
        xsl = xres[:, tc * 1024:(tc + 1) * 1024]
        e.v.tensor_tensor(s2[:, :], s2[:, :], xsl, ALU.add)
        h_out = ln.emit(e, s2[:, :], xsl)
    io["done"] = h_out
    return io


def standalone_block(emit_fn, weight_specs, debug_specs=None):
    p = Prog()
    x_h = p.dram_in("x", [T, C], F32)
    y_h = p.dram_out("y", [T, C], F32)
    io = {}
    if debug_specs:
        io["dbg"] = {n: p.dram_out("dbg_" + n, shape, dt) for n, shape, dt in debug_specs}
    p.prep_consts(*getattr(emit_fn, "CONSTS", ()))
    for name, shape in weight_specs:
        io[name] = p.dram_in(name.replace("_h", ""), shape, F32)
    xres = p.sb("xres", [128, NT * 1024], F32)
    p.prep_consts(LN_EPS, -0.5, ident=True, ones_f32=True)

    def body(e):
        io["consts"] = p.emit_consts(e)
        for w in (e.v, e.s, e.t):
            w.wait(io["consts"])
        xt = x_h.rearrange("(n p) c -> n p c", p=128)
        h = None
        for tc in range(NT):
            h = e.q.dma(xres[:, tc * 1024:(tc + 1) * 1024], xt[tc])
        io["xres"] = xres
        io["x_ready"] = h
        io["x_hbm"] = x_h
        emit_fn(p, e, io)
        e.gq.wait(io["done"])
        yt = y_h.rearrange("(n p) c -> n p c", p=128)
        for tc in range(NT):
            hs = e.gq.dma(yt[tc], xres[:, tc * 1024:(tc + 1) * 1024])
        e.gq.wait(hs)

    return p.run(body)



H = 16
HD = 64
SCALE = 0.125
LEAKY = 0.02
MA_K = 0.02


def emit_l0attn(p, e, io):
    nc = p.nc
    xres = io["xres"]
    h_x = io["x_ready"]
    wa_h = io["W_attn_h"]
    wk2_h = io["W_k2_h"]
    wp_h = io["W_proj_h"]
    alp = nc.allow_low_precision

    xT = p.sb("xT", [128, NC * T], BF16)
    yTsum = p.sb("yTsum", [128, NC * T], BF16)
    wstage = p.sb("wstage", [128, 1024], F32)
    wq_bf = p.sb("wqbf", [128, 1024], BF16)
    wk_bf = p.sb("wkbf", [128, 1024], BF16)
    w2_bf = p.sb("w2bf", [128, 1024], BF16)
    wp_bf = p.sb("wpbf", [128, 1024], BF16)
    qsT2 = p.sb("qsT2", [128, T], BF16)
    kT2 = p.sb("kT2", [128, T], BF16)
    kaT2 = p.sb("kaT2", [128, T], BF16)
    qaT2 = p.sb("qaT2", [128, T], BF16)
    kaug = p.sb("kaug", [128, T], BF16)
    qaug = p.sb("qaug", [128, T], BF16)
    ewt = p.sb("ewt", [128, 2 * 512], BF16)
    vaug = p.sb("vaug", [128, NT * 65], BF16)
    e_nat = p.sb("e_nat", [128, NT * HD], BF16)
    yT_h = p.sb("yT_h", [64, T], BF16)
    exS = p.sb("exS", [128, 512], BF16)
    mq_row = p.sb("mq_row", [1, T], F32)
    m_sc = p.sb("m_sc", [128, 8], F32)
    nm = p.sb("nm", [128, 1], F32)
    rsum = p.sb("rsum", [1, 512], F32)
    rsrep = p.sb("rsrep", [64, 512], F32)
    maskaddQ = p.sb("maskaddQ", [128, 128], F32)
    maskbuf = p.sb("maskbuf", [128, 4 * 516], BF16)
    mask_ns = [maskbuf[:, i * 516 + 1: i * 516 + 513] for i in range(4)]
    mask_st = [maskbuf[:, i * 516: i * 516 + 512] for i in range(4)]
    identf = p.sb("identf", [128, 128], F32)
    s2 = p.sb("s2", [128, 1024], F32)
    ln = io.get("ln") or LNKit(p)

    P = [p.ps(f"pool{i}", [128, 512], F32) for i in range(7)]
    psE = p.ps("psE", [128, 256], BF16)

    e.g.memset(maskaddQ[:, :], 0.0)
    e.g.affine_select(maskaddQ[:, :], maskaddQ[:, :], [[-1, 128]], ALU.is_ge, -1e9,
                      base=0, channel_multiplier=1)
    for i, off in enumerate((0, 128, 256, 384)):
        e.g.memset(maskbuf[:, i * 516: i * 516 + 1], 0.0)
        sl_ns = mask_ns[i]
        e.g.memset(sl_ns, 1.0)
        e.g.affine_select(sl_ns, sl_ns, [[1, 512]], ALU.is_ge, 0.0,
                          base=-off, channel_multiplier=-1)
    e.g.memset(identf[:, :], 1.0)
    h_msk = e.g.affine_select(identf[:, :], identf[:, :], [[-1, 128]], ALU.is_equal,
                              0.0, base=0, channel_multiplier=1)

    e.t.waits(h_x, io["consts"], h_msk)
    e.v.wait(h_msk)
    e.s.wait(h_msk)
    h_xT_ev = []
    ntp = 0
    for tc in range(NT):
        for j in range(NC):
            if ntp >= 2:
                e.t.wait(h_xT_ev[ntp - 2])
            ht = e.t.transpose(P[ntp % 2][:, 0:128],
                               xres[:, tc * 1024 + j * 128: tc * 1024 + (j + 1) * 128],
                               identf[:, :])
            e.s.wait(ht)
            with alp(reason="bf16 matmul operands"):
                h_xT_ev.append(e.s.copy(xT[:, j * T + tc * 128: j * T + (tc + 1) * 128],
                                        P[ntp % 2][:, 0:128]))
            ntp += 1
    h_xT = h_xT_ev[-1]

    e.v.wait(h_x)
    e.v.memset(vaug[:, :], 1.0)

    h_out = None
    h_prev_cast = None
    for hp in range(8):
        h_casts = []
        for (hbm, cols, dst) in ((wa_h, 128 * hp, wq_bf),
                                 (wa_h, 1024 + 128 * hp, wk_bf),
                                 (wk2_h, 128 * hp, w2_bf)):
            if h_prev_cast is not None:
                e.q.wait(h_prev_cast)
            if hp > 0:
                e.v.wait(e.t.milestone())
            src = hbm[:, cols:cols + 128].rearrange("(n p) c -> p n c", p=128)
            hw = e.q.dma(wstage[:, :].rearrange("p (n c) -> p n c", c=128), src)
            e.v.wait(hw)
            with alp(reason="bf16 matmul operands"):
                h_prev_cast = e.v.tensor_copy(dst[:, :], wstage[:, :])
            h_casts.append(h_prev_cast)

        e.t.wait(h_xT)
        h_exS_reader = None
        for wi, (wbf, dst, mode) in enumerate(((wq_bf, qsT2, "qs"),
                                               (wk_bf, kT2, "k"),
                                               (w2_bf, kaT2, "ka"))):
            e.t.wait(h_casts[wi])
            for nq in range(4):
                pb = P[4 + nq % 2]
                for k in range(NC):
                    hmm = e.t.matmul(pb[:, :], wbf[:, k * 128:(k + 1) * 128],
                                     xT[:, k * T + nq * 512: k * T + (nq + 1) * 512],
                                     start=(k == 0), stop=(k == NC - 1))
                e.s.wait(hmm)
                dsl = dst[:, nq * 512:(nq + 1) * 512]
                with alp(reason="bf16 matmul operands"):
                    if mode == "qs":
                        hv = e.s.activation(dsl, pb[:, :], AF.Copy, bias=0.0,
                                            scale=p.cval(SCALE))
                    elif mode == "ka":
                        if h_exS_reader is not None:
                            e.s.wait(h_exS_reader)
                        hv = e.s.activation(exS[:, :], pb[:, :], AF.Tanh, bias=0.0,
                                            scale=p.cval(0.5 * MA_K * SCALE))
                        e.v.wait(hv)
                        hv = e.v.tensor_scalar(dsl, exS[:, :], 0.5, 0.5,
                                               ALU.mult, ALU.add)
                        h_exS_reader = hv
                    else:
                        hv = e.s.copy(dsl, pb[:, :])
                e.t.wait(hv)
        for nq in range(4):
            if h_exS_reader is not None:
                pass
            with alp(reason="bf16"):
                e.v.tensor_scalar(exS[:, :], qsT2[:, nq * 512:(nq + 1) * 512],
                                  LEAKY, None, ALU.mult)
                e.v.tensor_tensor(qaT2[:, nq * 512:(nq + 1) * 512],
                                  qsT2[:, nq * 512:(nq + 1) * 512], exS[:, :], ALU.min)

        for sub in range(2):
            h = 2 * hp + sub
            rb = 64 * sub
            e.t.waits(e.v.milestone(), e.s.milestone())
            for qq in range(NT):
                kend = (qq + 1) * 128
                nch = (kend + 511) // 512
                for j in range(nch):
                    n0 = j * 512
                    n1 = min(kend, n0 + 512)
                    h_mmA = e.t.matmul(P[j][:, 0:n1 - n0],
                                       qsT2[rb:rb + 64, qq * 128:(qq + 1) * 128],
                                       kT2[rb:rb + 64, n0:n1],
                                       start=True, stop=True)
                j0 = qq // 4
                off = (qq % 4) * 128
                e.v.wait(h_mmA)
                e.v.tensor_tensor(s2[:, 0:128], P[j0][:, off:off + 128],
                                  maskaddQ[:, :], ALU.add)
                npart = 0
                for j in range(j0):
                    e.v.reduce_max(m_sc[:, npart:npart + 1], P[j][:, :],
                                   axis=AX.X, negate=True)
                    npart += 1
                if off > 0:
                    e.v.reduce_max(m_sc[:, npart:npart + 1], P[j0][:, 0:off],
                                   axis=AX.X, negate=True)
                    npart += 1
                e.v.reduce_max(m_sc[:, npart:npart + 1], s2[:, 0:128],
                               axis=AX.X, negate=True)
                npart += 1
                h_nm = e.v.tensor_reduce(nm[:, :], m_sc[:, 0:npart], axis=AX.X,
                                         op=ALU.min)
                e.t.wait(h_nm)
                htr = e.t.transpose(P[4][0:1, 0:128], nm[:, :], identf[:, :])
                e.s.wait(htr)
                h_mq = e.s.copy(mq_row[:, qq * 128:(qq + 1) * 128], P[4][0:1, 0:128])
                e.t.wait(h_mq)
            e.v.wait(h_mq)
            e.v.tensor_copy(kaug[0:64, :], kT2[rb:rb + 64, :])
            e.v.memset(kaug[64:65, :], 1.0)
            e.v.tensor_copy(qaug[0:64, :], qsT2[rb:rb + 64, :])
            with alp(reason="softmax shift in bf16 (exact: shift-invariant)"):
                h_aug = e.v.tensor_copy(qaug[64:65, :], mq_row[:, :])
            for kt in range(NT):
                with alp(reason="bf16 matmul operands"):
                    h_va = e.v.tensor_copy(
                        vaug[:, kt * 65: kt * 65 + 64],
                        xres[:, kt * 1024 + HD * h: kt * 1024 + HD * h + 64])
            for qc in range(4):
                ktmax = 4 * qc + 4
                h_ev = {}
                h_c = {}
                for kt in range(ktmax):
                    e.t.waits(h_aug, h_va)
                    hmm = e.t.matmul(P[4 + kt % 2][:, :],
                                     kaug[0:65, kt * 128:(kt + 1) * 128],
                                     qaug[0:65, qc * 512:(qc + 1) * 512],
                                     start=True, stop=True)
                    e.s.wait(hmm)
                    dst = ewt[:, (kt % 2) * 512:(kt % 2 + 1) * 512]
                    with alp(reason="bf16 matmul operands"):
                        if kt >= 4 * qc:
                            if h_exS_reader is not None:
                                e.s.wait(h_exS_reader)
                            hx = e.s.activation(exS[:, :], P[4 + kt % 2][:, :], AF.Exp,
                                                bias=0.0, scale=1.0)
                            if kt >= 2:
                                e.v.wait(h_c[kt - 2])
                            e.v.wait(hx)
                            off_i = kt - 4 * qc
                            h_ev[kt] = e.v.tensor_tensor(
                                dst, exS[:, :],
                                mask_ns[off_i], ALU.mult)
                            h_exS_reader = h_ev[kt]
                        else:
                            if kt >= 2:
                                e.s.wait(h_c[kt - 2])
                            h_ev[kt] = e.s.activation(dst, P[4 + kt % 2][:, :], AF.Exp,
                                                      bias=0.0, scale=1.0)
                    e.t.wait(h_ev[kt])
                    h_av = h_c[kt] = e.t.matmul(
                        P[6][0:65, :], vaug[:, kt * 65:(kt + 1) * 65], dst,
                        start=(kt == 0), stop=(kt == ktmax - 1))
                e.v.wait(h_av)
                h_rs = e.v.reciprocal(rsum[:, :], P[6][64:65, :])
                e.t.wait(h_rs)
                h_rep = e.t.matmul(P[3][0:64, :], p.onesf[0:1, 0:64], rsum[:, :],
                                   start=True, stop=True)
                e.s.wait(h_rep)
                h_rr = e.s.copy(rsrep[:, :], P[3][0:64, :])
                e.v.wait(h_rr)
                with alp(reason="bf16 matmul operands"):
                    h_yT = e.v.tensor_tensor(yT_h[:, qc * 512:(qc + 1) * 512],
                                             P[6][0:64, :], rsrep[:, :], ALU.mult)
                e.t.wait(h_yT)
            with alp(reason="bf16"):
                e.v.tensor_copy(qaug[0:64, :], xT_head_rows(xT, h))
                e.v.tensor_sub(kaug[0:64, 0:T - 1], qaug[0:64, 1:T],
                               yT_h[:, 0:T - 1])
            h_e = e.v.memset(kaug[0:64, T - 1:T], 0.0)
            e.t.wait(h_e)
            h_en = []
            for kt in range(NT):
                if kt >= 1:
                    e.t.wait(h_en[kt - 1])
                pe_sl = psE[:, (kt % 2) * 64:(kt % 2) * 64 + 64]
                ht = e.t.transpose(pe_sl, kaug[0:64, kt * 128:(kt + 1) * 128],
                                   p.identb[0:64, 0:64])
                e.v.wait(ht)
                h_en.append(e.v.tensor_copy(e_nat[:, kt * 64:(kt + 1) * 64], pe_sl))
            for qc in range(4):
                ktmax = 4 * qc + 4
                h_ev = {}
                h_c = {}
                for kt in range(ktmax):
                    hmm = e.t.matmul(P[4 + kt % 2][:, :],
                                     kaT2[rb:rb + 64, kt * 128:(kt + 1) * 128],
                                     qaT2[rb:rb + 64, qc * 512:(qc + 1) * 512],
                                     start=True, stop=True)
                    dst = ewt[:, (kt % 2) * 512:(kt % 2 + 1) * 512]
                    with alp(reason="bf16 matmul operands"):
                        if kt >= 4 * qc:
                            e.v.wait(hmm)
                            if kt >= 2:
                                e.v.wait(h_c[kt - 2])
                            off_i = kt - 4 * qc
                            h_ev[kt] = e.v.tensor_tensor(
                                dst, P[4 + kt % 2][:, :],
                                mask_st[off_i], ALU.mult)
                        else:
                            e.s.wait(hmm)
                            if kt >= 2:
                                e.s.wait(h_c[kt - 2])
                            h_ev[kt] = e.s.copy(dst, P[4 + kt % 2][:, :])
                    e.t.waits(h_ev[kt], h_en[-1])
                    h_y2 = h_c[kt] = e.t.matmul(
                        P[6][0:64, :], e_nat[:, kt * 64:(kt + 1) * 64], dst,
                        start=(kt == 0), stop=(kt == ktmax - 1))
                e.v.wait(h_y2)
                ytile = h // 2
                yrow = 64 * (h % 2)
                with alp(reason="bf16 matmul operands"):
                    h_acc = e.v.tensor_tensor(
                        yTsum[yrow:yrow + 64,
                              ytile * T + qc * 512: ytile * T + (qc + 1) * 512],
                        P[6][0:64, :], yT_h[:, qc * 512:(qc + 1) * 512], ALU.add)
                e.t.wait(h_acc)
            if io.get("dbg") is not None and h == 0:
                d = io["dbg"]
                e.gq.waits(e.v.milestone(), e.s.milestone())
                e.gq.dma(d["yT_h"], yT_h[:, :])
                e.gq.dma(d["e_nat"], e_nat[:, :])
                e.gq.dma(d["mq_row"], mq_row[:, :])
                e.gq.dma(d["qsT2"], qsT2[:, :])
                e.gq.dma(d["kT2"], kT2[:, :])
                e.gq.dma(d["kaT2"], kaT2[:, :])
                e.gq.dma(d["qaT2"], qaT2[:, :])
                e.gq.dma(d["ysum0"], yTsum[:, 0:T])

    wpt = wp_h.rearrange("(n p) c -> n p c", p=128)
    e.q.wait(e.v.milestone())
    for tcg in range(8):
        h_pr = None
        for k in range(NC):
            if h_prev_cast is not None:
                e.q.wait(h_prev_cast)
            hw = e.q.dma(wstage[:, :], wpt[k])
            if h_pr is not None:
                e.v.wait(h_pr)
            e.v.wait(hw)
            with alp(reason="bf16 matmul operands"):
                h_prev_cast = h_wp = e.v.tensor_copy(wp_bf[:, :], wstage[:, :])
            e.t.wait(h_wp)
            for tt in range(2):
                tc = tcg * 2 + tt
                for hh in range(2):
                    h_pr = e.t.matmul(P[2 * tt + hh][:, :],
                                      yTsum[:, k * T + tc * 128:k * T + (tc + 1) * 128],
                                      wp_bf[:, hh * 512:(hh + 1) * 512],
                                      start=(k == 0), stop=(k == NC - 1))
        e.v.wait(h_pr)
        for tt in range(2):
            tc = tcg * 2 + tt
            for hh in range(2):
                e.v.tensor_tensor(s2[:, hh * 512:(hh + 1) * 512], P[2 * tt + hh][:, :],
                                  xres[:, tc * 1024 + hh * 512:tc * 1024 + (hh + 1) * 512],
                                  ALU.add)
            h_out = ln.emit(e, s2[:, :], xres[:, tc * 1024:(tc + 1) * 1024])
        e.t.wait(h_out)
    io["done"] = h_out
    return io


def xT_head_rows(xT, h):
    tile = h // 2
    rb = 64 * (h % 2)
    return xT[rb:rb + 64, tile * T:(tile + 1) * T]


emit_l0attn.CONSTS = (SCALE, 0.5 * MA_K * SCALE)
emit_l1ffn.CONSTS = ()


ROUTE_SCALE = 2.5
NEXP = 16


def exS_f32(p):
    if not hasattr(p, "_exs_f32"):
        p._exs_f32 = p.sb("exsf32", [128, 8], F32)
    return p._exs_f32


def emit_l0ffn(p, e, io):
    nc = p.nc
    io["comb_sp"] = p.dram_out(f"comb_sp_{p.uid()}", [T, 128], BF16)
    xres = io["xres"]
    h_x = io["x_ready"]
    x_hbm = io["x_hbm"]
    alp = nc.allow_low_precision

    xT = p.sb("mxT", [128, NC * T], BF16)
    wstage = p.sb("mwstage", [128, 1024], F32)
    wg_bf = p.sb("mwg", [128, NC * 128], BF16)
    wu_bf = p.sb("mwu", [128, NC * 128], BF16)
    w2_bf = p.sb("mw2", [128, 4 * 1024], BF16)
    gw_bf = p.sb("mgw", [128, NC * 16], BF16)
    gb_rep = p.sb("mgb", [128, 16], F32)
    gb_row = p.sb("mgbrow", [1, 16], F32)
    scores = p.sb("mscores", [128, 16], F32)
    biased = p.sb("mbiased", [128, 16], F32)
    work = p.sb("mwork", [128, 16], F32)
    mx = p.sb("mmx", [128, 1], F32)
    msk1 = p.sb("mmsk", [128, 16], F32)
    t4 = p.sb("mt4", [128, 1], F32)
    wsel = p.sb("mwsel", [128, 16], F32)
    rsum = p.sb("mrsum", [128, 1], F32)
    comb = p.sb("mcomb", [128, NT * 16], F32)
    combTbig = p.sb("mcombT", [128, T], BF16)
    combT = combTbig[0:16, :]
    combbf = p.sb("mcombbf", [128, 256], BF16)
    combrep = p.sb("mcombrep", [128, T], BF16)
    Rsel = p.sb("mRsel", [16, 16 * 128], BF16)
    scT = p.sb("mscT", [128, 4 * T], BF16)
    gsb = p.sb("mgsb", [128, 512], BF16)
    usb = p.sb("musb", [128, 512], BF16)
    ssb = p.sb("mssb", [128, 512], BF16)
    xtmp = p.sb("mxtmp", [128, 1024], F32)
    s2 = p.sb("ms2", [128, 1024], F32)
    ln = LNKit(p)

    P = [p.ps(f"mpool{i}", [128, 512], F32) for i in range(8)]

    e.g.memset(Rsel[:, :], 1.0)
    h_rsel = e.g.affine_select(Rsel[:, :], Rsel[:, :], [[-1, 16], [0, 128]],
                               ALU.is_equal, 0.0, base=0, channel_multiplier=1)

    identf = p.sb("midentf", [128, 128], F32)
    e.g.memset(identf[:, :], 1.0)
    h_idf = e.g.affine_select(identf[:, :], identf[:, :], [[-1, 128]], ALU.is_equal,
                              0.0, base=0, channel_multiplier=1)
    e.t.waits(h_x, io["consts"], h_idf, h_rsel)
    e.v.waits(h_idf, h_rsel)
    e.s.wait(h_idf)
    if "xT_h" in io:
        xTt = io["xT_h"].rearrange("(n p) t -> n p t", p=128)
        h_xT = None
        for j in range(NC):
            for half in range(2):
                hw = e.q.dma(wstage[:, :], xTt[j][:, half * 1024:(half + 1) * 1024])
                e.v.wait(hw)
                with nc.allow_low_precision(reason="bf16 matmul operands"):
                    h_xT = e.v.tensor_copy(
                        xT[:, j * T + half * 1024: j * T + (half + 1) * 1024],
                        wstage[:, :])
                e.q.wait(h_xT)
    else:
        h_ev = []
        ntp = 0
        for tc in range(NT):
            for j in range(NC):
                if ntp >= 2:
                    e.t.wait(h_ev[ntp - 2])
                ht = e.t.transpose(P[ntp % 2][:, 0:128],
                                   xres[:, tc * 1024 + j * 128: tc * 1024 + (j + 1) * 128],
                                   identf[:, :])
                e.s.wait(ht)
                with nc.allow_low_precision(reason="bf16 matmul operands"):
                    h_ev.append(e.s.copy(xT[:, j * T + tc * 128: j * T + (tc + 1) * 128],
                                         P[ntp % 2][:, 0:128]))
                ntp += 1
        h_xT = h_ev[-1]

    if io.get("dbg") is not None:
        e.gq.waits(h_xT)
        e.gq.dma(io["dbg"]["xT1"], xT[:, 0:2048])
        e.gq.dma(io["dbg"]["xT1b"], xT[:, 7 * T:8 * T])
    h_gw = e.q.dma(wstage[:, 0:128].rearrange("p (n c) -> p n c", c=16),
                   io["gate_wT_h"].rearrange("(n p) c -> p n c", p=128))
    e.v.wait(h_gw)
    with alp(reason="bf16 matmul operands"):
        h_gwc = e.v.tensor_copy(gw_bf[:, :], wstage[:, 0:128])
    h_gb = e.q.dma(gb_row[:, :], io["gate_b_h"][:, :])
    e.t.waits(h_gb, io["consts"])
    h_gbm = e.t.matmul(P[2][0:128, 0:16], p.onesf[:, :], gb_row[:, :],
                       start=True, stop=True)
    e.s.wait(h_gbm)
    h_gbr = e.s.copy(gb_rep[:, :], P[2][:, 0:16])

    if io.get("dbg") is not None:
        io["_dbg_sc"] = p.sb("dbgsc", [128, 256], F32)
        io["_dbg_bi"] = p.sb("dbgbi", [128, 256], F32)
        io["_dbg_ws"] = p.sb("dbgws", [128, 256], F32)
        io["_dbg_rs"] = p.sb("dbgrs", [128, 16], F32)

    e.g.wait(h_xT)
    h_z = None
    for tc in range(NT):
        h_z = e.g.memset(xres[:, tc * 1024:(tc + 1) * 1024], 0.0)
    e.t.wait(h_z)
    scraw = p.sb("mscraw", [128, NT * 16], F32)
    scraw2 = p.sb("mscraw2", [128, NT * 16], F32)
    e.t.waits(h_xT, h_gwc)
    h_tnl = []
    for tc in range(NT):
        for k in range(NC):
            h_rt = e.t.matmul(P[3][:, 0:16], xT[:, k * T + tc * 128:k * T + (tc + 1) * 128],
                              gw_bf[:, k * 16:(k + 1) * 16],
                              start=(k == 0), stop=(k == NC - 1))
        e.s.wait(h_rt)
        h_tnl.append(e.s.activation(scraw[:, tc * 16:(tc + 1) * 16], P[3][:, 0:16],
                                    AF.Tanh, bias=0.0, scale=p.cval(0.5)))
        e.t.wait(h_tnl[-1])
    io["_route2"] = (scraw, scraw2, h_tnl[-1])
    for tc in range(NT if False else 0):
        pass
    def route_stage2(tc):
        e.v.tensor_scalar(scores[:, :], scraw2[:, tc * 16:(tc + 1) * 16], 0.5, 0.5,
                          ALU.mult, ALU.add)
        h_sc = e.v.tensor_tensor(biased[:, :], scores[:, :], gb_rep[:, :], ALU.add)
        e.v.tensor_copy(work[:, :], biased[:, :])
        for r in range(4):
            e.v.reduce_max(t4[:, :] if r == 3 else mx[:, :], work[:, :], axis=AX.X)
            if r < 3:
                e.v.tensor_tensor(msk1[:, :], work[:, :],
                                  mx[:, :].broadcast_to([128, 16]), ALU.is_ge)
                e.v.scalar_tensor_tensor(work[:, :], msk1[:, :], -1e9, work[:, :],
                                         ALU.mult, ALU.add)
        e.v.tensor_tensor(msk1[:, :], biased[:, :],
                          t4[:, :].broadcast_to([128, 16]), ALU.is_ge)
        e.v.tensor_tensor(wsel[:, :], scores[:, :], msk1[:, :], ALU.mult)
        e.v.reduce_sum(rsum[:, :], wsel[:, :], axis=AX.X)
        e.v.reciprocal(mx[:, :], rsum[:, :])
        e.v.tensor_tensor(work[:, :], wsel[:, :],
                          mx[:, :].broadcast_to([128, 16]), ALU.mult)
        e.v.tensor_scalar(comb[:, tc * 16:(tc + 1) * 16], work[:, :],
                          ROUTE_SCALE, None, ALU.mult)
        if io.get("dbg") is not None:
            e.v.tensor_copy(io["_dbg_sc"][:, tc * 16:(tc + 1) * 16], scores[:, :])
            e.v.tensor_copy(io["_dbg_bi"][:, tc * 16:(tc + 1) * 16], biased[:, :])
            e.v.tensor_copy(io["_dbg_ws"][:, tc * 16:(tc + 1) * 16], wsel[:, :])
            e.v.tensor_copy(io["_dbg_rs"][:, tc:tc + 1], rsum[:, :])
    h_cT = []

    def emit_stage2_and_combT():
        e.v.wait(h_tnl[-1])
        e.v.tensor_copy(scraw2[:, :], scraw[:, :])
        for tc in range(NT):
            route_stage2(tc)
        with alp(reason="bf16 matmul operands"):
            h_cb = e.v.tensor_copy(combbf[:, :], comb[:, :])
        e.gq.wait(h_cb)
        hsp = None
        for tc in range(NT):
            hsp = e.gq.dma(io["comb_sp"].rearrange("(n p) e -> n p e", p=128)[tc][:, 0:16],
                           combbf[:, tc * 16:(tc + 1) * 16])
        e.q.wait(hsp)
        inst = nc.sync.dma_start_transpose(combTbig[:, :], io["comb_sp"][:, :])
        inst.then_inc(e.q.sem, 16)
        e.q.n += 16
        h_cT.append((e.q.sem, e.q.n))
        e.v.wait(h_cT[-1])

    w13T_h = io["w13T_h"]
    w2T_h = io["w2T_h"]
    w13sT_h = io["w13_sT_h"]
    w2sT_h = io["w2_sT_h"]
    h_wst_reader = h_gwc
    h_acc = {}
    h_sct = None
    for idx, ei in enumerate([NEXP] + list(range(NEXP))):
        shared = ei == NEXP
        if idx == 1:
            emit_stage2_and_combT()
        if io.get("BISECT_CONST_COMBREP") and not shared:
            e.t.waits(h_cT[-1], h_z)
            h_crep = e.v.memset(combrep[:, :], 0.1)
        elif not shared:
            e.t.waits(h_cT[-1], h_z)
            h_crm = e.t.matmul(P[2][:, 0:T // 4], Rsel[:, ei * 128:(ei + 1) * 128],
                               combT[:, 0:512], start=True, stop=True)
            hs = []
            for quarter in range(4):
                if quarter > 0:
                    h_crm = e.t.matmul(P[2][:, 0:512], Rsel[:, ei * 128:(ei + 1) * 128],
                                       combT[:, quarter * 512:(quarter + 1) * 512],
                                       start=True, stop=True)
                e.v.wait(h_crm)
                with alp(reason="bf16"):
                    hs.append(e.v.tensor_copy(combrep[:, quarter * 512:(quarter + 1) * 512],
                                              P[2][:, 0:512]))
                e.t.wait(hs[-1])
            h_crep = hs[-1]
        for fc in range(4):
            for half, dstw in ((0, wg_bf), (1, wu_bf)):
                col = (half * 512) + fc * 128
                if shared:
                    src = w13sT_h[:, col:col + 128].rearrange("(n p) c -> p n c", p=128)
                else:
                    src = w13T_h[ei][:, col:col + 128].rearrange("(n p) c -> p n c", p=128)
                e.q.wait(h_wst_reader)
                hw = e.q.dma(wstage[:, :].rearrange("p (n c) -> p n c", c=128), src)
                e.v.wait(hw)
                with alp(reason="bf16 matmul operands"):
                    h_wst_reader = e.v.tensor_copy(dstw[:, :], wstage[:, :])
            e.t.wait(h_wst_reader)
            for tq in range(4):
                pg = P[4 + tq % 2]
                pu = P[6 + tq % 2]
                for k in range(NC):
                    e.t.matmul(pg[:, :], wg_bf[:, k * 128:(k + 1) * 128],
                               xT[:, k * T + tq * 512:k * T + (tq + 1) * 512],
                               start=(k == 0), stop=(k == NC - 1))
                for k in range(NC):
                    h_gu = e.t.matmul(pu[:, :], wu_bf[:, k * 128:(k + 1) * 128],
                                      xT[:, k * T + tq * 512:k * T + (tq + 1) * 512],
                                      start=(k == 0), stop=(k == NC - 1))
                e.s.wait(h_gu)
                if h_sct is not None:
                    e.s.wait(h_sct)
                with alp(reason="bf16 matmul operands"):
                    h_si = e.s.activation(gsb[:, :], pg[:, :], AF.Silu, bias=0.0, scale=1.0)
                    h_uc = e.s.copy(usb[:, :], pu[:, :])
                e.v.waits(h_si, h_uc)
                if not shared:
                    e.v.wait(h_crep)
                dst = scT[:, fc * T + tq * 512: fc * T + (tq + 1) * 512]
                with alp(reason="bf16 matmul operands"):
                    e.v.tensor_tensor(ssb[:, :], gsb[:, :], usb[:, :], ALU.mult)
                    if shared:
                        h_sct = e.v.tensor_copy(dst, ssb[:, :])
                    else:
                        h_sct = e.v.tensor_tensor(
                            dst, ssb[:, :], combrep[:, tq * 512:(tq + 1) * 512], ALU.mult)
                e.t.wait(h_sct)
        for fc in range(4):
            if shared:
                src = w2sT_h[fc * 128:(fc + 1) * 128, :]
            else:
                src = w2T_h[ei][fc * 128:(fc + 1) * 128, :]
            e.q.wait(h_wst_reader)
            hw = e.q.dma(wstage[:, :], src)
            e.v.wait(hw)
            with alp(reason="bf16 matmul operands"):
                h_wst_reader = e.v.tensor_copy(w2_bf[:, fc * 1024:(fc + 1) * 1024],
                                               wstage[:, :])
        e.t.wait(h_wst_reader)
        if io.get("dbg") is not None and ei == 0:
            e.gq.waits(e.v.milestone(), e.s.milestone())
            e.gq.dma(io["dbg"]["scT0"], scT[:, :])
            e.gq.dma(io["dbg"]["combrep0"], combrep[:, :])
        for tc in range(NT):
            pa, pb_ = P[4 + 2 * (tc % 2)], P[5 + 2 * (tc % 2)]
            if tc >= 2:
                e.t.wait(h_acc[tc - 2])
            elif idx > 0:
                e.t.wait(h_acc[NT - 2 + tc])
            for fc in range(4):
                for hh in range(2):
                    h_ym = e.t.matmul((pa if hh == 0 else pb_)[:, :],
                                      scT[:, fc * T + tc * 128: fc * T + (tc + 1) * 128],
                                      w2_bf[:, fc * 1024 + hh * 512: fc * 1024 + (hh + 1) * 512],
                                      start=(fc == 0), stop=(fc == 3))
            e.v.wait(h_ym)
            with alp(reason="moe accumulate"):
                e.v.tensor_tensor(xres[:, tc * 1024:tc * 1024 + 512],
                                  pa[:, :], xres[:, tc * 1024:tc * 1024 + 512], ALU.add)
                h_acc[tc] = e.v.tensor_tensor(
                    xres[:, tc * 1024 + 512:(tc + 1) * 1024],
                    pb_[:, :], xres[:, tc * 1024 + 512:(tc + 1) * 1024], ALU.add)
    if io.get("dbg") is not None:
        d = io["dbg"]
        e.gq.waits(e.v.milestone(), e.s.milestone())
        e.gq.dma(d["scall"], io["_dbg_sc"][:, :])
        e.gq.dma(d["biall"], io["_dbg_bi"][:, :])
        e.gq.dma(d["wsall"], io["_dbg_ws"][:, :])
        e.gq.dma(d["rsall"], io["_dbg_rs"][:, :])
        e.gq.dma(d["cpool"], p._ctile[:, :])
        e.gq.dma(d["scraw"], scraw[:, :])
        e.gq.dma(d["scraw2"], scraw2[:, :])
        e.gq.dma(d["comb"], comb[:, :])
        e.gq.dma(d["combT"], combT[:, :])
        e.gq.dma(io["dbg"]["yacc"], xres[:, :])
    xt = x_hbm.rearrange("(n p) c -> n p c", p=128)
    h_out = None
    for tc in range(NT):
        hx = e.q.dma(xtmp[:, :], xt[tc])
        e.v.waits(hx, h_acc.get(NT - 2), h_acc.get(NT - 1))
        e.v.tensor_tensor(s2[:, :], xres[:, tc * 1024:(tc + 1) * 1024], xtmp[:, :],
                          ALU.add)
        h_out = ln.emit(e, s2[:, :], xres[:, tc * 1024:(tc + 1) * 1024])
        e.q.wait(h_out)
    io["done"] = h_out
    return io


emit_l0ffn.CONSTS = (0.5,)


DIN = 2048
DST = 16
DTR = 64


def emit_l1attn(p, e, io):
    nc = p.nc
    xres = io["xres"]
    h_x = io["x_ready"]
    x_hbm = io["x_hbm"]
    alp = nc.allow_low_precision

    bufB = p.sb("abufB", [128, 16 * T], BF16)
    repc = xres[:, :].bitcast(BF16)
    dbcT = p.sb("adbcT", [128, T], F32)
    dltT = p.sb("adltT", [64, T], BF16)
    BCbf = p.sb("aBC", [32, T], BF16)
    Rsel32 = p.sb("aRsel", [32, 32 * 128], BF16)
    wxp = p.sb("awxp", [128, 16 * 96], BF16)
    wdt = p.sb("awdt", [64, T], BF16)
    acols = p.sb("aacols", [128, 16 * 16], F32)
    dcols = p.sb("adcols", [128, 16], F32)
    bdtcols = p.sb("abdt", [128, 16], F32)
    wconvc = p.sb("awconv", [128, 64], F32)
    wstage = p.sb("awstage", [128, 1024], F32)
    wchunk = p.sb("awchunk", [128, 1024], BF16)
    convsrc = p.sb("aconvsrc", [128, T + 4], BF16)
    acc1 = p.sb("aacc1", [128, T], BF16)
    acc2 = p.sb("aacc2", [128, T], BF16)
    zst = p.sb("azst", [128, T], BF16)
    xcst = p.sb("axcst", [128, T], BF16)
    xch = p.sb("axch", [128, 1024], BF16)
    zsh = p.sb("azsh", [128, 1024], BF16)
    uTd = p.sb("auTd", [128, 1024], BF16)
    dA = p.sb("adA", [128, 1024], BF16)
    dBx = p.sb("adBx", [128, 1024], BF16)
    hsc = p.sb("ahsc", [128, 1024], BF16)
    hC = p.sb("ahC", [128, 1024], BF16)
    hlast = p.sb("ahlast", [128, 256], F32)
    s2 = dbcT[:, 0:1024]
    epi4 = dbcT[:, 1024:2048]
    identf = p.sb("aidentf", [128, 128], F32)
    ln = LNKit(p)

    PA = [p.ps(f"apsA{i}", [128, 512], F32) for i in range(2)]
    PB = p.ps("apsB", [128, T], F32)
    PC = p.ps("apsC", [128, 1024], F32)

    xc_sp = p.dram_out(f"xc_sp_{p.uid()}", [16, 128, T], BF16)
    zs_sp = p.dram_out(f"zs_sp_{p.uid()}", [16, 128, T], BF16)

    e.g.memset(identf[:, :], 1.0)
    e.g.affine_select(identf[:, :], identf[:, :], [[-1, 128]], ALU.is_equal, 0.0,
                      base=0, channel_multiplier=1)
    e.g.memset(Rsel32[:, :], 1.0)
    h_gp = e.g.affine_select(Rsel32[:, :], Rsel32[:, :], [[-1, 32], [0, 128]],
                             ALU.is_equal, 0.0, base=0, channel_multiplier=1)
    e.g.memset(convsrc[:, 0:4], 0.0)

    h_ac = e.q.dma(acols[:, :], io["A_log_cols_h"][:, :])
    h_dc = e.q.dma(dcols[:, :], io["D_cols_h"][:, :])
    h_bc = e.q.dma(bdtcols[:, :], io["bdt_cols_h"][:, :])
    h_wc = e.q.dma(wconvc[:, :], io["Wconv_cols_h"][:, :])
    for g in range(2):
        h_wx = e.q.dma(wstage[:, 0:768].rearrange("p (n c) -> p n c", c=96),
                       io["W_xprojT_h"][g * 1024:(g + 1) * 1024, :]
                       .rearrange("(n p) c -> p n c", p=128))
        e.v.waits(h_wx, io["consts"])
        with alp(reason="bf16 matmul operands"):
            h_wxc = e.v.tensor_copy(wxp[:, g * 768:(g + 1) * 768], wstage[:, 0:768])
        e.q.wait(h_wxc)
    for half in range(2):
        h_wd = e.q.dma(wstage[0:64, :], io["W_dtT_h"][:, half * 1024:(half + 1) * 1024])
        e.v.wait(h_wd)
        with alp(reason="bf16 matmul operands"):
            h_wdc = e.v.tensor_copy(wdt[:, half * 1024:(half + 1) * 1024],
                                    wstage[0:64, :])
        e.q.wait(h_wdc)
    e.s.waits(h_ac, io["consts"])
    aneg = acc1[:, :].bitcast(F32)
    h_a1 = e.s.activation(aneg[:, 0:256], acols[:, :], AF.Exp, bias=0.0, scale=1.0)
    e.v.wait(h_a1)
    h_a2 = e.v.tensor_scalar(acols[:, :], aneg[:, 0:256], -1.0, None, ALU.mult)

    xT = bufB[:, 0:NC * T]
    e.t.waits(h_x, io["consts"], h_gp)
    e.v.wait(h_gp)
    e.s.wait(h_gp)
    h_xT_ev = []
    ntp = 0
    for tc in range(NT):
        for j in range(NC):
            if ntp >= 2:
                e.t.wait(h_xT_ev[ntp - 2])
            ht = e.t.transpose(PA[ntp % 2][:, 0:128],
                               xres[:, tc * 1024 + j * 128: tc * 1024 + (j + 1) * 128],
                               identf[:, :])
            e.s.wait(ht)
            with alp(reason="bf16 matmul operands"):
                h_xT_ev.append(e.s.copy(xT[:, j * T + tc * 128: j * T + (tc + 1) * 128],
                                        PA[ntp % 2][:, 0:128]))
            ntp += 1
    h_xT = h_xT_ev[-1]

    winT = io["W_inT_h"]
    h_wst_reader = h_wdc
    h_dbc_mm = None
    h_spill = {}
    for fi in range(32):
        is_z = fi >= 16
        d = fi - 16 if is_z else fi
        e.q.wait(h_wst_reader)
        src = winT[:, fi * 128:(fi + 1) * 128].rearrange("(n p) c -> p n c", p=128)
        hw = e.q.dma(wstage[:, :].rearrange("p (n c) -> p n c", c=128), src)
        e.v.wait(hw)
        with alp(reason="bf16 matmul operands"):
            h_wst_reader = e.v.tensor_copy(wchunk[:, :], wstage[:, :])
        e.t.waits(h_wst_reader, h_xT)
        for tq in range(4):
            pbk = PA[tq % 2]
            for k in range(NC):
                hmm = e.t.matmul(pbk[:, :], wchunk[:, k * 128:(k + 1) * 128],
                                 xT[:, k * T + tq * 512:k * T + (tq + 1) * 512],
                                 start=(k == 0), stop=(k == NC - 1))
            e.s.wait(hmm)
            hsp = h_spill.get(fi - 1)
            if hsp is not None:
                e.s.wait(hsp)
            with alp(reason="bf16 matmul operands"):
                if is_z:
                    h_t1 = e.s.activation(acc1[:, 0:512], pbk[:, :], AF.Tanh,
                                          bias=0.0, scale=p.cval(0.5))
                    h_t2 = e.s.copy(acc2[:, 0:512], pbk[:, :])
                    e.v.waits(h_t1, h_t2)
                    e.v.tensor_scalar(acc1[:, 512:1024], acc1[:, 0:512], 0.5, 0.5,
                                      ALU.mult, ALU.add)
                    h_ev = e.v.tensor_tensor(zst[:, tq * 512:(tq + 1) * 512],
                                             acc2[:, 0:512], acc1[:, 512:1024], ALU.mult)
                else:
                    h_ev = e.s.copy(convsrc[:, 4 + tq * 512: 4 + (tq + 1) * 512],
                                    pbk[:, :])
            e.t.wait(h_ev)
        if is_z:
            e.q.wait(h_ev)
            h_spill[fi] = e.q.dma(zs_sp[d], zst[:, :])
        else:
            e.v.waits(h_ev, h_wc)
            with alp(reason="bf16"):
                e.v.tensor_scalar(acc1[:, :], convsrc[:, 1:1 + T],
                                  wconvc[:, 4 * d:4 * d + 1], None, ALU.mult)
                e.v.scalar_tensor_tensor(acc2[:, :], convsrc[:, 2:2 + T],
                                         wconvc[:, 4 * d + 1:4 * d + 2], acc1[:, :],
                                         ALU.mult, ALU.add)
                e.v.scalar_tensor_tensor(acc1[:, :], convsrc[:, 3:3 + T],
                                         wconvc[:, 4 * d + 2:4 * d + 3], acc2[:, :],
                                         ALU.mult, ALU.add)
                h_cv = e.v.scalar_tensor_tensor(acc2[:, :], convsrc[:, 4:4 + T],
                                                wconvc[:, 4 * d + 3:4 * d + 4], acc1[:, :],
                                                ALU.mult, ALU.add)
            e.s.wait(h_cv)
            hsp = h_spill.get(fi - 1)
            if hsp is not None:
                e.s.wait(hsp)
            if h_dbc_mm is not None:
                e.s.wait(h_dbc_mm)
                e.v.wait(h_dbc_mm)
            h_tn = e.s.activation(acc1[:, :], acc2[:, :], AF.Tanh, bias=0.0,
                                  scale=p.cval(0.5))
            e.v.wait(h_tn)
            with alp(reason="bf16"):
                e.v.tensor_scalar(acc1[:, :], acc1[:, :], 0.5, 0.5, ALU.mult, ALU.add)
                h_si = e.v.tensor_tensor(xcst[:, :], acc2[:, :], acc1[:, :], ALU.mult)
            e.t.waits(h_si, h_wxc)
            for nq in range(4):
                h_dbc_mm = e.t.matmul(PB[0:96, nq * 512:(nq + 1) * 512],
                                      wxp[:, d * 96:(d + 1) * 96],
                                      xcst[:, nq * 512:(nq + 1) * 512],
                                      start=(d == 0), stop=(d == 15))
            e.q.wait(h_si)
            h_spill[fi] = e.q.dma(xc_sp[d], xcst[:, :])
            if io.get("dbg") is not None and fi == 0:
                e.gq.waits(e.s.milestone(), h_spill[0])
                e.gq.dma(io["dbg"]["xcst0"], xcst[:, :])
                e.gq.dma(io["dbg"]["xcsp0"], xc_sp[0])

    e.v.wait(h_dbc_mm)
    h_dbc = None
    for nq in range(4):
        h_dbc = e.v.tensor_copy(dbcT[0:96, nq * 512:(nq + 1) * 512],
                                PB[0:96, nq * 512:(nq + 1) * 512])
    with alp(reason="bf16 matmul operands"):
        h_dlt = e.v.tensor_copy(dltT[:, :], dbcT[0:64, :])
        h_bcc = e.v.tensor_copy(BCbf[:, :], dbcT[64:96, :])

    h_dmm = {}
    for d in range(NT):
        e.t.waits(h_dlt, h_wdc)
        if d >= 1:
            e.t.wait(h_dmm[d - 1])
        for nq in range(4):
            hm = e.t.matmul(PB[:, nq * 512:(nq + 1) * 512], wdt[:, d * 128:(d + 1) * 128],
                            dltT[:, nq * 512:(nq + 1) * 512],
                            start=True, stop=True)
        e.v.waits(hm, h_bc)
        for hh in range(2):
            xin = s2
            e.v.tensor_scalar(xin, PB[:, hh * 1024:(hh + 1) * 1024],
                              bdtcols[:, d:d + 1], None, ALU.add)
            True
            m_ = acc1[:, :].bitcast(F32)
            arg = acc2[:, :].bitcast(F32)
            e.v.tensor_scalar(m_, xin, 0.0, None, ALU.max)
            h_arg = e.v.scalar_tensor_tensor(arg, m_, -2.0, xin, ALU.mult, ALU.add)
            e.s.wait(h_arg)
            h_ex = e.s.activation(xin, arg, AF.Exp, bias=0.0, scale=1.0)
            h_ln = e.s.activation(arg, xin, AF.Ln, bias=p.cval(1.0), scale=1.0)
            e.v.wait(h_ln)
            with alp(reason="bf16 delta"):
                h_dmm[d] = e.v.tensor_tensor(
                    bufB[:, d * T + hh * 1024: d * T + (hh + 1) * 1024], m_, arg, ALU.add)

    if io.get("dbg") is not None:
        d_ = io["dbg"]
        e.gq.waits(e.v.milestone(), e.s.milestone())
        e.gq.dma(d_["dbcT"], dbcT[:, :])
        e.gq.dma(d_["delta0"], bufB[:, 0:T])
        e.gq.dma(d_["delta7"], bufB[:, 7 * T:8 * T])
        e.gq.dma(d_["xc0"], xc_sp[0])
        e.gq.dma(d_["zs0"], zs_sp[0])
    h_out = None
    h_rep_reader = None
    h_hl_w = {}
    h_scan_prev = None
    h_accum_prev = None
    for th in range(2):
        t0 = th * 1024
        e.t.waits(h_bcc, e.v.milestone())
        h_repev = []
        for idx in range(32):
            if h_rep_reader is not None:
                e.t.wait(h_rep_reader)
            for hh in range(2):
                hr = e.t.matmul(PB[:, hh * 512:(hh + 1) * 512],
                                Rsel32[:, idx * 128:(idx + 1) * 128],
                                BCbf[:, t0 + hh * 512:t0 + (hh + 1) * 512],
                                start=True, stop=True)
            e.v.wait(hr)
            with alp(reason="bf16"):
                h_repev.append(e.v.tensor_copy(repc[:, idx * 1024:(idx + 1) * 1024],
                                               PB[:, 0:1024]))
            e.t.wait(h_repev[-1])
        h_rep = h_repev[-1]
        for d in range(NT):
            e.q.wait(h_spill[d])
            hx1 = e.q.dma(xch[:, :], xc_sp[d][:, t0:t0 + 1024])
            e.v.waits(hx1, h_dmm.get(NT - 1), h_rep)
            with alp(reason="bf16"):
                e.v.tensor_tensor(uTd[:, :], bufB[:, d * T + t0: d * T + t0 + 1024],
                                  xch[:, :], ALU.mult)
            for n in range(16):
                if h_scan_prev is not None:
                    e.s.wait(h_scan_prev)
                h_da = e.s.activation(dA[:, :], bufB[:, d * T + t0: d * T + t0 + 1024],
                                      AF.Exp, bias=0.0, scale=acols[:, d * 16 + n:d * 16 + n + 1])
                with alp(reason="bf16"):
                    h_db = e.v.tensor_tensor(dBx[:, :], uTd[:, :],
                                             repc[:, n * 1024:(n + 1) * 1024], ALU.mult)
                e.v.wait(h_da)
                ini = 0.0 if th == 0 else hlast[:, d * 16 + n:d * 16 + n + 1]
                with alp(reason="bf16 scan"):
                    h_h = e.v.tensor_tensor_scan(hsc[:, :], dA[:, :], dBx[:, :], ini,
                                                 ALU.mult, ALU.add)
                h_scan_prev = h_h
                if io.get("dbg") is not None and th == 0 and d == 0 and n == 15:
                    e.gq.wait(h_h)
                    e.gq.dma(io["dbg"]["dA00"], dA[:, :])
                    e.gq.dma(io["dbg"]["dBx00"], dBx[:, :])
                    e.gq.dma(io["dbg"]["h00"], hsc[:, :])
                    e.gq.dma(io["dbg"]["uTd0"], uTd[:, :])
                    e.gq.dma(io["dbg"]["brep0"], repc[:, 15 * 1024:16 * 1024])
                    e.gq.dma(io["dbg"]["acols"], acols[:, :])
                    hdmp = e.gq.dma(io["dbg"]["crep0"], repc[:, 31 * 1024:32 * 1024])
                    e.v.wait(hdmp)
                    e.s.wait(hdmp)
                if th == 0:
                    h_hl_w[(d, n)] = e.v.tensor_copy(hlast[:, d * 16 + n:d * 16 + n + 1],
                                                     hsc[:, 1023:1024])
                if h_accum_prev is not None:
                    e.v.wait(h_accum_prev)
                with alp(reason="bf16"):
                    h_hc = e.v.tensor_tensor(hC[:, :], hsc[:, :],
                                             repc[:, (16 + n) * 1024:(17 + n) * 1024],
                                             ALU.mult)
                e.t.wait(h_hc)
                for hh in range(2):
                    h_acc = e.t.matmul(PC[:, hh * 512:(hh + 1) * 512], p.identb[:, :],
                                       hC[:, hh * 512:(hh + 1) * 512],
                                       start=(n == 0), stop=(n == 15))
                h_accum_prev = h_acc
            e.q.wait(h_spill[16 + d])
            hz1 = e.q.dma(zsh[:, :], zs_sp[d][:, t0:t0 + 1024])
            e.v.waits(h_acc, hz1)
            e.v.scalar_tensor_tensor(s2, xch[:, :], dcols[:, d:d + 1], PC[:, :],
                                     ALU.mult, ALU.add)
            with alp(reason="bf16"):
                h_yf = e.v.tensor_tensor(bufB[:, d * T + t0: d * T + t0 + 1024],
                                         s2, zsh[:, :], ALU.mult)
            e.t.wait(h_yf)
            e.q.wait(h_yf)
            h_rep_reader = h_yf

    if io.get("dbg") is not None:
        e.gq.waits(e.v.milestone(), e.s.milestone())
        e.gq.dma(io["dbg"]["yf0"], bufB[:, 0:T])
        e.gq.dma(io["dbg"]["hlast"], hlast[:, :])
    wout_slots = [acc1[:, 0:1024], acc1[:, 1024:2048], acc2[:, 0:1024],
                  acc2[:, 1024:2048], zst[:, 0:1024], zst[:, 1024:2048],
                  xcst[:, 0:1024], xcst[:, 1024:2048], convsrc[:, 4:1028],
                  convsrc[:, 1028:2052], dA[:, :], dBx[:, :], hsc[:, :], hC[:, :],
                  uTd[:, :], wchunk[:, 0:1024]]
    woutT = io["W_outT_h"]
    h_wo = []
    for k in range(NT):
        e.q.wait(h_rep_reader if k == 0 else h_wo[-1])
        hw = e.q.dma(wstage[:, :], woutT.rearrange("(n p) c -> n p c", p=128)[k])
        e.v.wait(hw)
        with alp(reason="bf16 matmul operands"):
            h_wo.append(e.v.tensor_copy(wout_slots[k], wstage[:, :]))
    xt = x_hbm.rearrange("(n p) c -> n p c", p=128)
    e.t.wait(h_wo[-1])
    h_ep = None
    for tc in range(NT):
        if h_ep is not None:
            e.t.wait(h_ep)
        for k in range(NT):
            for hh in range(2):
                hm = e.t.matmul(PC[:, hh * 512:(hh + 1) * 512],
                                bufB[:, k * T + tc * 128:k * T + (tc + 1) * 128],
                                wout_slots[k][:, hh * 512:(hh + 1) * 512],
                                start=(k == 0), stop=(k == NT - 1))
        hx2 = e.q.dma(epi4, xt[tc])
        e.v.waits(hm, hx2)
        e.v.tensor_tensor(s2, PC[:, :], epi4, ALU.add)
        h_out = ln.emit(e, s2, xres[:, tc * 1024:(tc + 1) * 1024])
        h_ep = h_out
        e.q.wait(h_out)
    io["done"] = h_out
    return io


emit_l1attn.CONSTS = (1.0, 0.5)


def fused_program():
    p = Prog()
    x_h = p.dram_in("x", [T, C], F32)
    y_h = p.dram_out("y", [T, C], F32)
    w = {}
    for name, shape in (("W_attn_h", [1024, 2048]), ("W_k2_h", [1024, 1024]),
                        ("W_proj_h", [1024, 1024]), ("gate_wT_h", [1024, 16]),
                        ("gate_b_h", [1, 16]), ("w13T_h", [16, 1024, 1024]),
                        ("w2T_h", [16, 512, 1024]), ("w13_sT_h", [1024, 1024]),
                        ("w2_sT_h", [512, 1024]), ("W_inT_h", [1024, 4096]),
                        ("W_xprojT_h", [2048, 96]), ("W_dtT_h", [64, 2048]),
                        ("W_outT_h", [2048, 1024]), ("A_log_cols_h", [128, 256]),
                        ("D_cols_h", [128, 16]), ("bdt_cols_h", [128, 16]),
                        ("Wconv_cols_h", [128, 64]), ("weight_h", [1024, 1024]),
                        ("bias_h", [1, 1024])):
        w[name] = p.dram_in(name[:-2], shape)
    scr0 = p.dram_out("xspill0", [T, C], F32)
    scr1 = p.dram_out("xspill1", [T, C], F32)
    xres = p.sb("xres", [128, NT * 1024], F32)
    p.prep_consts(LN_EPS, -0.5, SCALE, 0.5 * MA_K * SCALE, 0.5, 1.0,
                  ident=True, ones_f32=True)

    def body(e):
        consts = p.emit_consts(e)
        allw = (e.v, e.s, e.t, e.g, e.q, e.gq, e.sq)
        for wr in (e.v, e.s, e.t):
            wr.wait(consts)

        def full_barrier():
            for a in allw:
                for b in allw:
                    if b.n > 0 and b.sem is not a.sem:
                        a.wait((b.sem, b.n))

        xt = x_h.rearrange("(n p) c -> n p c", p=128)
        h = None
        for tc in range(NT):
            h = e.q.dma(xres[:, tc * 1024:(tc + 1) * 1024], xt[tc])

        io0 = {"xres": xres, "x_ready": h, "consts": consts,
               "W_attn_h": w["W_attn_h"], "W_k2_h": w["W_k2_h"],
               "W_proj_h": w["W_proj_h"]}
        p.push_scope()
        emit_l0attn(p, e, io0)
        full_barrier()
        p.pop_scope()

        s0 = scr0.rearrange("(n p) c -> n p c", p=128)
        hs = None
        for tc in range(NT):
            hs = e.sq.dma(s0[tc], xres[:, tc * 1024:(tc + 1) * 1024])
        io1 = {"xres": xres, "x_ready": hs, "consts": consts, "x_hbm": scr0,
               "gate_wT_h": w["gate_wT_h"], "gate_b_h": w["gate_b_h"],
               "w13T_h": w["w13T_h"], "w2T_h": w["w2T_h"],
               "w13_sT_h": w["w13_sT_h"], "w2_sT_h": w["w2_sT_h"]}
        p.push_scope()
        emit_l0ffn(p, e, io1)
        full_barrier()
        p.pop_scope()

        s1 = scr1.rearrange("(n p) c -> n p c", p=128)
        hs = None
        for tc in range(NT):
            hs = e.sq.dma(s1[tc], xres[:, tc * 1024:(tc + 1) * 1024])
        io2 = {"xres": xres, "x_ready": hs, "consts": consts, "x_hbm": scr1,
               "W_inT_h": w["W_inT_h"], "W_xprojT_h": w["W_xprojT_h"],
               "W_dtT_h": w["W_dtT_h"], "W_outT_h": w["W_outT_h"],
               "A_log_cols_h": w["A_log_cols_h"], "D_cols_h": w["D_cols_h"],
               "bdt_cols_h": w["bdt_cols_h"], "Wconv_cols_h": w["Wconv_cols_h"]}
        p.push_scope()
        emit_l1attn(p, e, io2)
        full_barrier()
        p.pop_scope()

        io3 = {"xres": xres, "x_ready": io2["done"], "consts": consts,
               "weight_h": w["weight_h"], "bias_h": w["bias_h"]}
        p.push_scope()
        emit_l1ffn(p, e, io3)
        full_barrier()
        p.pop_scope()

        e.gq.wait(io3["done"])
        yt = y_h.rearrange("(n p) c -> n p c", p=128)
        for tc in range(NT):
            hfin = e.gq.dma(yt[tc], xres[:, tc * 1024:(tc + 1) * 1024])
        e.gq.wait(hfin)

    return p.run(body)


_PROGS = {}


def _prog_l0attn():
    if "l0a" not in _PROGS:
        _PROGS["l0a"] = standalone_block(
            emit_l0attn,
            [("W_attn_h", [1024, 2048]), ("W_k2_h", [1024, 1024]),
             ("W_proj_h", [1024, 1024])])
    return _PROGS["l0a"]


def _prog_l0ffn():
    if "l0f" not in _PROGS:
        _PROGS["l0f"] = standalone_block(
            emit_l0ffn,
            [("gate_wT_h", [1024, 16]), ("gate_b_h", [1, 16]),
             ("w13T_h", [16, 1024, 1024]), ("w2T_h", [16, 512, 1024]),
             ("w13_sT_h", [1024, 1024]), ("w2_sT_h", [512, 1024]),
             ("xT_h", [1024, 2048])])
    return _PROGS["l0f"]


def _prog_l1attn():
    if "l1a" not in _PROGS:
        _PROGS["l1a"] = standalone_block(
            emit_l1attn,
            [("W_inT_h", [1024, 4096]), ("W_xprojT_h", [2048, 96]),
             ("W_dtT_h", [64, 2048]), ("W_outT_h", [2048, 1024]),
             ("A_log_cols_h", [128, 256]), ("D_cols_h", [128, 16]),
             ("bdt_cols_h", [128, 16]), ("Wconv_cols_h", [128, 64])])
    return _PROGS["l1a"]


def _prog_l1ffn():
    if "l1f" not in _PROGS:
        _PROGS["l1f"] = standalone_block(
            emit_l1ffn,
            [("weight_h", [1024, 1024]), ("bias_h", [1, 1024])])
    return _PROGS["l1f"]


def _launch(nc, in_maps):
    res = run_bass_kernel_spmd(nc, in_maps, core_ids=list(range(8)))
    return [r["y"] for r in res.results]


def kernel(L0_attn_x, L0_attn_W_attn, L0_attn_W_k2, L0_attn_W_proj,
           L0_ffn_input_ids, L0_ffn_gate_w, L0_ffn_gate_b, L0_ffn_w13,
           L0_ffn_w2, L0_ffn_w13_s, L0_ffn_w2_s, L1_attn_W_in, L1_attn_W_conv,
           L1_attn_W_xproj, L1_attn_W_dt, L1_attn_b_dt, L1_attn_A_log,
           L1_attn_D, L1_attn_W_out, L1_ffn_weight, L1_ffn_bias):
    f32 = np.float32
    _ = np.asarray(L0_ffn_input_ids)
    x = np.ascontiguousarray(np.asarray(L0_attn_x, dtype=f32))

    Wa = np.ascontiguousarray(np.asarray(L0_attn_W_attn, f32))
    Wk2 = np.ascontiguousarray(np.asarray(L0_attn_W_k2, f32))
    Wp = np.ascontiguousarray(np.asarray(L0_attn_W_proj, f32))
    gwT = np.ascontiguousarray(np.asarray(L0_ffn_gate_w, f32).T)
    gb = np.ascontiguousarray(np.asarray(L0_ffn_gate_b, f32).reshape(1, 16))
    w13T = np.ascontiguousarray(np.asarray(L0_ffn_w13, f32).transpose(0, 2, 1))
    w2T = np.ascontiguousarray(np.asarray(L0_ffn_w2, f32).transpose(0, 2, 1))
    w13sT = np.ascontiguousarray(np.asarray(L0_ffn_w13_s, f32).T)
    w2sT = np.ascontiguousarray(np.asarray(L0_ffn_w2_s, f32).T)
    W_inT = np.ascontiguousarray(np.asarray(L1_attn_W_in, f32).T)
    W_xprojT = np.ascontiguousarray(np.asarray(L1_attn_W_xproj, f32).T)
    W_dtT = np.ascontiguousarray(np.asarray(L1_attn_W_dt, f32).T)
    W_outT = np.ascontiguousarray(np.asarray(L1_attn_W_out, f32).T)
    A_log = np.asarray(L1_attn_A_log, f32)
    A_cols = np.ascontiguousarray(
        A_log.reshape(16, 128, 16).transpose(1, 0, 2).reshape(128, 256))
    D_cols = np.ascontiguousarray(np.asarray(L1_attn_D, f32).reshape(16, 128).T)
    bdt_cols = np.ascontiguousarray(np.asarray(L1_attn_b_dt, f32).reshape(16, 128).T)
    Wconv_cols = np.ascontiguousarray(
        np.asarray(L1_attn_W_conv, f32).reshape(16, 128, 4)
        .transpose(1, 0, 2).reshape(128, 64))
    wt = np.ascontiguousarray(np.asarray(L1_ffn_weight, f32))
    bias = np.ascontiguousarray(np.asarray(L1_ffn_bias, f32).reshape(1, 1024))

    if "fused" not in _PROGS:
        _PROGS["fused"] = fused_program()
    maps = [{
        "x": x[b], "W_attn": Wa, "W_k2": Wk2, "W_proj": Wp,
        "gate_wT": gwT, "gate_b": gb, "w13T": w13T, "w2T": w2T,
        "w13_sT": w13sT, "w2_sT": w2sT,
        "W_inT": W_inT, "W_xprojT": W_xprojT, "W_dtT": W_dtT, "W_outT": W_outT,
        "A_log_cols": A_cols, "D_cols": D_cols, "bdt_cols": bdt_cols,
        "Wconv_cols": Wconv_cols, "weight": wt, "bias": bias,
    } for b in range(8)]
    ys = _launch(_PROGS["fused"], maps)
    return np.stack(ys, axis=0).astype(np.float32)


# version 05
# speedup vs baseline: 1.0414x; 1.0414x over previous
import contextlib
import numpy as np
import ml_dtypes
import concourse.bass as bass
import concourse.mybir as mybir
from concourse.bass_utils import run_bass_kernel_spmd

F32 = mybir.dt.float32
BF16 = mybir.dt.bfloat16
AF = mybir.ActivationFunctionType
ALU = mybir.AluOpType
AX = mybir.AxisListType
FMAGIC = 12582912.0
T = 2048
C = 1024
NT = T // 128
NC = C // 128
LN_EPS = 1e-5


class W:

    def __init__(self, p, eng, name, flush=True, dma_inc=None):
        self.p = p
        self.e = eng
        self.name = name
        self.sem = p.sem("ch_" + name)
        self.n = 0
        self.flush = flush
        self.dma_inc = dma_inc
        self._waited = {}

    def wait(self, handle):
        sem, val = handle
        if sem is self.sem:
            pass
        key = id(sem)
        if self._waited.get(key, -1) >= val:
            return
        self.e.wait_ge(sem, val)
        self._waited[key] = val

    def waits(self, *handles):
        for h in handles:
            if h is not None:
                self.wait(h)

    def milestone(self):
        return (self.sem, self.n)

    def _op(self, fn, a, k):
        if self.flush and self.n > 0:
            self.wait((self.sem, self.n))
        inst = fn(*a, **k)
        inc = self.dma_inc or 1
        inst.then_inc(self.sem, inc)
        self.n += inc
        return (self.sem, self.n)

    def __getattr__(self, name):
        fn = getattr(self.e, name)

        def call(*a, **k):
            return self._op(fn, a, k)
        return call

    def dma(self, out, in_, **k):
        inst = self.e.dma_start(out=out, in_=in_, **k)
        inst.then_inc(self.sem, 16)
        self.n += 16
        return (self.sem, self.n)


class E:

    def __init__(self, p):
        self.v = W(p, p.nc.vector, "v")
        self.s = W(p, p.nc.scalar, "s")
        self.g = W(p, p.nc.gpsimd, "g")
        self.t = W(p, p.nc.tensor, "t", flush=False)
        self.q = W(p, p.nc.sync, "q")
        self.gq = W(p, p.nc.gpsimd, "gq")
        self.sq = W(p, p.nc.scalar, "sq")


class Prog:
    def __init__(self, name="prog"):
        self.nc = bass.Bass("TRN2", target_bir_lowering=False, debug=False)
        self.ctx = contextlib.ExitStack()
        self._uid = 0
        self._consts = {}
        self._ctile = None
        self._need_ident = False
        self._need_ones_f32 = False
        self._need_ones_bf = False

    def uid(self):
        self._uid += 1
        return self._uid

    def sb(self, name, shape, dt=F32):
        stack = self._scopes[-1] if getattr(self, "_scopes", None) else self.ctx
        return stack.enter_context(self.nc.sbuf_tensor(f"{name}_{self.uid()}", shape, dt))

    def ps(self, name, shape, dt=F32):
        stack = self._scopes[-1] if getattr(self, "_scopes", None) else self.ctx
        return stack.enter_context(self.nc.psum_tensor(f"{name}_{self.uid()}", shape, dt))

    def push_scope(self):
        if not hasattr(self, "_scopes"):
            self._scopes = []
        self._scopes.append(contextlib.ExitStack())

    def pop_scope(self):
        self._scopes.pop().close()

    def sem(self, name):
        h = self.ctx.enter_context(self.nc.semaphore(f"{name}_{self.uid()}"))
        if not hasattr(self, "_my_sems"):
            self._my_sems = []
        self._my_sems.append(h)
        return h

    def dram_in(self, name, shape, dt=F32):
        return self.nc.dram_tensor(name, shape, dt, kind="ExternalInput").ap()

    def dram_out(self, name, shape, dt=F32):
        return self.nc.dram_tensor(name, shape, dt, kind="ExternalOutput").ap()

    def dram_scratch(self, name, shape, dt=F32):
        return self.nc.dram_tensor(f"{name}_{self.uid()}", shape, dt, kind="Internal").ap()

    def cval(self, v):
        if self._ctile is None:
            self._ctile = self.sb("constpool", [128, 16], F32)
        if v not in self._consts:
            assert not getattr(self, "_consts_emitted", False), \
                f"const {v} registered after emit_consts -- add it to the emitter's CONSTS"
            assert len(self._consts) < 16
            self._consts[v] = len(self._consts)
        return self._ctile[:, self._consts[v]:self._consts[v] + 1]

    def prep_consts(self, *vals, ident=False, ones_f32=False, ones_bf=False):
        for v in vals:
            self.cval(v)
        if ident:
            self._need_ident = True
            self.identb = self.sb("identb", [128, 128], BF16)
        if ones_f32:
            self._need_ones_f32 = True
            self.onesf = self.sb("onesf", [1, 128], F32)
        if ones_bf:
            self._need_ones_bf = True
            self.onesb = self.sb("onesb", [1, 128], BF16)

    def emit_consts(self, e):
        self._consts_emitted = True
        for v, i in self._consts.items():
            e.g.memset(self._ctile[:, i:i + 1], float(v))
        if self._need_ident:
            e.g.memset(self.identb[:, :], 1.0)
            e.g.affine_select(self.identb[:, :], self.identb[:, :], [[-1, 128]],
                              ALU.is_equal, 0.0, base=0, channel_multiplier=1)
        if self._need_ones_f32:
            e.g.memset(self.onesf[:, :], 1.0)
        if self._need_ones_bf:
            e.g.memset(self.onesb[:, :], 1.0)
        return e.g.milestone()

    def run(self, body):
        from concourse.bass import get_kernel_semaphore_range
        with self.nc.Block() as blk:
            @blk.sync
            def _(sync):
                e = E(self)
                body(e)
                g = self.nc.gpsimd
                for w in (e.v, e.s, e.t, e.g, e.q, e.gq, e.sq):
                    if w.n > 0 and w.sem is not self.nc.gpsimd:
                        g.wait_ge(w.sem, w.n)
                nums = sorted(s.num for s in self._my_sems)
                lo, hi = nums[0], nums[-1] + 1
                assert nums == list(range(lo, hi)), nums
                g.dma_reset(range(lo, hi))
                g.sem_clear(range(lo, hi))
        self.ctx.close()
        return self.nc


class QuantKit:
    def __init__(self, p, width=1024):
        self.p = p
        self.width = width
        nb = width // 32
        self.amax = p.sb("q_amax", [128, nb], F32)
        self.rs = p.sb("q_rs", [128, nb], F32)
        self.t = p.sb("q_t", [128, width], F32)
        self.a = p.sb("q_a", [128, width], F32)
        self.m1 = p.sb("q_m1", [128, width], F32)
        self.m2 = p.sb("q_m2", [128, width], F32)
        self.v1 = p.sb("q_v1", [128, width], F32)

    def emit(self, v, src_f32, dst_bf, scale_sl, width=None):
        w = width or self.width
        nb = w // 32
        r32 = lambda ap: ap.rearrange("p (b j) -> p b j", j=32)
        sl = lambda tile: tile[:, 0:w]
        v.reduce_max(self.amax[:, 0:nb].unsqueeze(2), r32(src_f32),
                     axis=AX.X, apply_absolute_value=True)
        v.tensor_scalar(scale_sl, self.amax[:, 0:nb], 1.0 / 7.5, 1e-30, ALU.mult, ALU.max)
        v.reciprocal(self.rs[:, 0:nb], scale_sl)
        rsrep = self.rs[:, 0:nb].unsqueeze(2).broadcast_to([128, nb, 32])
        screp = scale_sl.unsqueeze(2).broadcast_to([128, nb, 32])
        v.tensor_tensor(r32(sl(self.t)), r32(src_f32), rsrep, ALU.mult)
        v.tensor_tensor(sl(self.a), sl(self.t), sl(self.t), ALU.mult)
        v.tensor_scalar(sl(self.m1), sl(self.a), 4.0, None, ALU.is_ge)
        v.tensor_scalar(sl(self.m2), sl(self.a), 16.0, None, ALU.is_ge)
        v.tensor_scalar(sl(self.a), sl(self.m2), -2.0, 8.0, ALU.mult, ALU.add)
        v.scalar_tensor_tensor(sl(self.v1), sl(self.m1), -4.0, sl(self.a),
                               ALU.mult, ALU.add)
        v.tensor_tensor(sl(self.a), sl(self.t), sl(self.v1), ALU.mult)
        v.tensor_scalar(sl(self.v1), sl(self.a), FMAGIC, None, ALU.add)
        v.tensor_scalar(sl(self.a), sl(self.v1), FMAGIC, None, ALU.subtract)
        v.tensor_scalar(sl(self.v1), sl(self.m2), 0.25, 0.125, ALU.mult, ALU.add)
        v.scalar_tensor_tensor(sl(self.m2), sl(self.m1), 0.125, sl(self.v1),
                               ALU.mult, ALU.add)
        v.tensor_tensor(r32(sl(self.m1)), r32(sl(self.m2)), screp, ALU.mult)
        with self.p.nc.allow_low_precision(reason="bf16 matmul operand"):
            out = v.tensor_tensor(dst_bf, sl(self.a), sl(self.m1), ALU.mult)
        return out


class LNKit:
    def __init__(self, p):
        self.p = p
        self.bn12 = p.sb("ln_bn12", [128, 12], F32)
        self.mv = p.sb("ln_mv", [128, 2], F32)
        self.rstd = p.sb("ln_rstd", [128, 1], F32)

    def emit(self, e, s2, dst):
        v, s = e.v, e.s
        v.bn_stats(self.bn12[:, 0:6], s2[:, 0:512])
        v.bn_stats(self.bn12[:, 6:12], s2[:, 512:1024])
        h_mv = v.bn_aggr(self.mv[:, :], self.bn12[:, :])
        s.wait(h_mv)
        s.activation(self.rstd[:, :], self.mv[:, 1:2], AF.Ln,
                     bias=self.p.cval(LN_EPS), scale=1.0)
        h_r = s.activation(self.rstd[:, :], self.rstd[:, :], AF.Exp, bias=0.0,
                           scale=self.p.cval(-0.5))
        v.wait(h_r)
        return v.tensor_scalar(dst, s2, self.mv[:, 0:1], self.rstd[:, :],
                               ALU.subtract, ALU.mult)


def emit_l1ffn(p, e, io):
    nc = p.nc
    xres = io["xres"]
    h_x = io["x_ready"]

    xd = p.sb("xd", [128, NT * 1024], BF16)
    xdT = p.sb("xdT", [128, NC * T], BF16)
    wnat = p.sb("wnat", [128, 1024], F32)
    wd = p.sb("wd", [128, NC * 1024], BF16)
    wdT = p.sb("wdT", [128, NC * 1024], BF16)
    bias_row = p.sb("biasrow", [1, 1024], F32)
    biasrep = p.sb("biasrep", [128, 1024], F32)
    q_scale = p.sb("q_scale", [128, NT * 32], F32)
    s2 = p.sb("s2", [128, 1024], F32)
    qk = QuantKit(p)
    ln = LNKit(p)

    ps_rep = [p.ps(f"psrep{i}", [128, 512], F32) for i in range(2)]
    ps_tp = [p.ps(f"pstp{i}", [128, 128], BF16) for i in range(2)]
    ps_mm = [p.ps(f"psmm{i}", [128, 512], F32) for i in range(2)]

    h_b = e.q.dma(bias_row[:, :], io["bias_h"][:, :])
    e.t.waits(h_b, io["consts"])
    h_repmm = [e.t.matmul(ps_rep[h][:, :], p.onesf[:, :],
                          bias_row[:, h * 512:(h + 1) * 512], start=True, stop=True)
               for h in range(2)]
    for h in range(2):
        e.s.wait(h_repmm[h])
        e.s.copy(biasrep[:, h * 512:(h + 1) * 512], ps_rep[h][:, :])

    e.v.wait(h_x)
    h_xq = []
    for tc in range(NT):
        h_xq.append(qk.emit(e.v, xres[:, tc * 1024:(tc + 1) * 1024],
                            xd[:, tc * 1024:(tc + 1) * 1024],
                            q_scale[:, tc * 32:(tc + 1) * 32]))

    wt = io["weight_h"].rearrange("(n p) c -> n p c", p=128)
    h_wq = []
    for k in range(NC):
        if k >= 1:
            e.q.wait(h_wq[k - 1])
        h_wl = e.q.dma(wnat[:, :], wt[k])
        e.v.wait(h_wl)
        h_wq.append(qk.emit(e.v, wnat[:, :],
                            wd[:, k * 1024:(k + 1) * 1024],
                            q_scale[:, k * 32:(k + 1) * 32]))

    h_tp_ev = []
    ntp = 0
    for k in range(NC):
        e.t.wait(h_wq[k])
        for j in range(NC):
            if ntp >= 2:
                e.t.wait(h_tp_ev[ntp - 2])
            h = e.t.transpose(ps_tp[ntp % 2][:, :],
                              wd[:, k * 1024 + j * 128: k * 1024 + (j + 1) * 128],
                              p.identb[:, :])
            e.s.wait(h)
            h_tp_ev.append(e.s.copy(wdT[:, j * 1024 + k * 128: j * 1024 + (k + 1) * 128],
                                    ps_tp[ntp % 2][:, :]))
            ntp += 1
    for tc in range(NT):
        e.t.wait(h_xq[tc])
        for j in range(NC):
            if ntp >= 2:
                e.t.wait(h_tp_ev[ntp - 2])
            h = e.t.transpose(ps_tp[ntp % 2][:, :],
                              xd[:, tc * 1024 + j * 128: tc * 1024 + (j + 1) * 128],
                              p.identb[:, :])
            e.s.wait(h)
            h_tp_ev.append(e.s.copy(xdT[:, j * T + tc * 128: j * T + (tc + 1) * 128],
                                    ps_tp[ntp % 2][:, :]))
            ntp += 1
    h_alltp = h_tp_ev[-1]

    e.t.wait(h_alltp)
    h_ev = {}
    h_out = None
    for tc in range(NT):
        for hh in range(2):
            if tc >= 1:
                e.t.wait(h_ev[(tc - 1, hh)])
            for k in range(NC):
                h_mm = e.t.matmul(ps_mm[hh][:, :],
                                  xdT[:, k * T + tc * 128: k * T + (tc + 1) * 128],
                                  wdT[:, k * 1024 + hh * 512: k * 1024 + (hh + 1) * 512],
                                  start=(k == 0), stop=(k == NC - 1))
            e.v.wait(h_mm)
            h_ev[(tc, hh)] = e.v.tensor_tensor(
                s2[:, hh * 512:(hh + 1) * 512], ps_mm[hh][:, :],
                biasrep[:, hh * 512:(hh + 1) * 512], ALU.add)
        xsl = xres[:, tc * 1024:(tc + 1) * 1024]
        e.v.tensor_tensor(s2[:, :], s2[:, :], xsl, ALU.add)
        h_out = ln.emit(e, s2[:, :], xsl)
    io["done"] = h_out
    return io


def standalone_block(emit_fn, weight_specs, debug_specs=None):
    p = Prog()
    x_h = p.dram_in("x", [T, C], F32)
    y_h = p.dram_out("y", [T, C], F32)
    io = {}
    if debug_specs:
        io["dbg"] = {n: p.dram_out("dbg_" + n, shape, dt) for n, shape, dt in debug_specs}
    p.prep_consts(*getattr(emit_fn, "CONSTS", ()))
    for name, shape in weight_specs:
        io[name] = p.dram_in(name.replace("_h", ""), shape, F32)
    xres = p.sb("xres", [128, NT * 1024], F32)
    p.prep_consts(LN_EPS, -0.5, ident=True, ones_f32=True)

    def body(e):
        io["consts"] = p.emit_consts(e)
        for w in (e.v, e.s, e.t):
            w.wait(io["consts"])
        xt = x_h.rearrange("(n p) c -> n p c", p=128)
        h = None
        for tc in range(NT):
            h = e.q.dma(xres[:, tc * 1024:(tc + 1) * 1024], xt[tc])
        io["xres"] = xres
        io["x_ready"] = h
        io["x_hbm"] = x_h
        emit_fn(p, e, io)
        e.gq.wait(io["done"])
        yt = y_h.rearrange("(n p) c -> n p c", p=128)
        for tc in range(NT):
            hs = e.gq.dma(yt[tc], xres[:, tc * 1024:(tc + 1) * 1024])
        e.gq.wait(hs)

    return p.run(body)



H = 16
HD = 64
SCALE = 0.125
LEAKY = 0.02
MA_K = 0.02


def emit_l0attn(p, e, io):
    nc = p.nc
    xres = io["xres"]
    h_x = io["x_ready"]
    wa_h = io["W_attn_h"]
    wk2_h = io["W_k2_h"]
    wp_h = io["W_proj_h"]
    alp = nc.allow_low_precision

    xT = p.sb("xT", [128, NC * T], BF16)
    yTsum = p.sb("yTsum", [128, NC * T], BF16)
    wstage = p.sb("wstage", [128, 1024], F32)
    wq_bf = p.sb("wqbf", [128, 1024], BF16)
    wk_bf = p.sb("wkbf", [128, 1024], BF16)
    w2_bf = p.sb("w2bf", [128, 1024], BF16)
    wp_bf = p.sb("wpbf", [128, 1024], BF16)
    qsT2 = p.sb("qsT2", [128, T], BF16)
    kT2 = p.sb("kT2", [128, T], BF16)
    kaT2 = p.sb("kaT2", [128, T], BF16)
    qaT2 = p.sb("qaT2", [128, T], BF16)
    kaug = p.sb("kaug", [128, T], BF16)
    qaug = p.sb("qaug", [128, T], BF16)
    ewt = p.sb("ewt", [128, 2 * 512], BF16)
    vaug = p.sb("vaug", [128, NT * 65], BF16)
    e_nat = p.sb("e_nat", [128, NT * HD], BF16)
    yT_h = p.sb("yT_h", [64, T], BF16)
    exS = p.sb("exS", [128, 512], BF16)
    mq_row = p.sb("mq_row", [1, T], F32)
    m_sc = p.sb("m_sc", [128, 8], F32)
    nm = p.sb("nm", [128, 1], F32)
    rsum = p.sb("rsum", [1, 512], F32)
    rsrep = p.sb("rsrep", [64, 512], F32)
    maskaddQ = p.sb("maskaddQ", [128, 128], F32)
    maskbuf = p.sb("maskbuf", [128, 4 * 516], BF16)
    mask_ns = [maskbuf[:, i * 516 + 1: i * 516 + 513] for i in range(4)]
    mask_st = [maskbuf[:, i * 516: i * 516 + 512] for i in range(4)]
    identf = p.sb("identf", [128, 128], F32)
    s2 = p.sb("s2", [128, 1024], F32)
    ln = io.get("ln") or LNKit(p)

    P = [p.ps(f"pool{i}", [128, 512], F32) for i in range(7)]
    psE = p.ps("psE", [128, 256], BF16)

    e.g.memset(maskaddQ[:, :], 0.0)
    e.g.affine_select(maskaddQ[:, :], maskaddQ[:, :], [[-1, 128]], ALU.is_ge, -1e9,
                      base=0, channel_multiplier=1)
    for i, off in enumerate((0, 128, 256, 384)):
        e.g.memset(maskbuf[:, i * 516: i * 516 + 1], 0.0)
        sl_ns = mask_ns[i]
        e.g.memset(sl_ns, 1.0)
        e.g.affine_select(sl_ns, sl_ns, [[1, 512]], ALU.is_ge, 0.0,
                          base=-off, channel_multiplier=-1)
    e.g.memset(identf[:, :], 1.0)
    h_msk = e.g.affine_select(identf[:, :], identf[:, :], [[-1, 128]], ALU.is_equal,
                              0.0, base=0, channel_multiplier=1)

    e.t.waits(h_x, io["consts"], h_msk)
    e.v.wait(h_msk)
    e.s.wait(h_msk)
    h_xT_ev = []
    ntp = 0
    for tc in range(NT):
        for j in range(NC):
            if ntp >= 2:
                e.t.wait(h_xT_ev[ntp - 2])
            ht = e.t.transpose(P[ntp % 2][:, 0:128],
                               xres[:, tc * 1024 + j * 128: tc * 1024 + (j + 1) * 128],
                               identf[:, :])
            e.s.wait(ht)
            with alp(reason="bf16 matmul operands"):
                h_xT_ev.append(e.s.copy(xT[:, j * T + tc * 128: j * T + (tc + 1) * 128],
                                        P[ntp % 2][:, 0:128]))
            ntp += 1
    h_xT = h_xT_ev[-1]

    e.v.wait(h_x)
    e.v.memset(vaug[:, :], 1.0)

    h_out = None
    h_prev_cast = None
    for hp in range(8):
        h_casts = []
        for (hbm, cols, dst) in ((wa_h, 128 * hp, wq_bf),
                                 (wa_h, 1024 + 128 * hp, wk_bf),
                                 (wk2_h, 128 * hp, w2_bf)):
            if h_prev_cast is not None:
                e.q.wait(h_prev_cast)
            if hp > 0:
                e.v.wait(e.t.milestone())
            src = hbm[:, cols:cols + 128].rearrange("(n p) c -> p n c", p=128)
            hw = e.q.dma(wstage[:, :].rearrange("p (n c) -> p n c", c=128), src)
            e.v.wait(hw)
            with alp(reason="bf16 matmul operands"):
                h_prev_cast = e.v.tensor_copy(dst[:, :], wstage[:, :])
            h_casts.append(h_prev_cast)

        e.t.wait(h_xT)
        h_exS_reader = None
        for wi, (wbf, dst, mode) in enumerate(((wq_bf, qsT2, "qs"),
                                               (wk_bf, kT2, "k"),
                                               (w2_bf, kaT2, "ka"))):
            e.t.wait(h_casts[wi])
            for nq in range(4):
                pb = P[4 + nq % 2]
                for k in range(NC):
                    hmm = e.t.matmul(pb[:, :], wbf[:, k * 128:(k + 1) * 128],
                                     xT[:, k * T + nq * 512: k * T + (nq + 1) * 512],
                                     start=(k == 0), stop=(k == NC - 1))
                e.s.wait(hmm)
                dsl = dst[:, nq * 512:(nq + 1) * 512]
                with alp(reason="bf16 matmul operands"):
                    if mode == "qs":
                        hv = e.s.activation(dsl, pb[:, :], AF.Copy, bias=0.0,
                                            scale=p.cval(SCALE))
                    elif mode == "ka":
                        if h_exS_reader is not None:
                            e.s.wait(h_exS_reader)
                        hv = e.s.activation(exS[:, :], pb[:, :], AF.Tanh, bias=0.0,
                                            scale=p.cval(0.5 * MA_K * SCALE))
                        e.v.wait(hv)
                        hv = e.v.tensor_scalar(dsl, exS[:, :], 0.5, 0.5,
                                               ALU.mult, ALU.add)
                        h_exS_reader = hv
                    else:
                        hv = e.s.copy(dsl, pb[:, :])
                e.t.wait(hv)
        for nq in range(4):
            if h_exS_reader is not None:
                pass
            with alp(reason="bf16"):
                e.v.tensor_scalar(exS[:, :], qsT2[:, nq * 512:(nq + 1) * 512],
                                  LEAKY, None, ALU.mult)
                e.v.tensor_tensor(qaT2[:, nq * 512:(nq + 1) * 512],
                                  qsT2[:, nq * 512:(nq + 1) * 512], exS[:, :], ALU.min)

        for sub in range(2):
            h = 2 * hp + sub
            rb = 64 * sub
            e.t.waits(e.v.milestone(), e.s.milestone())
            for qq in range(NT):
                kend = (qq + 1) * 128
                nch = (kend + 511) // 512
                for j in range(nch):
                    n0 = j * 512
                    n1 = min(kend, n0 + 512)
                    h_mmA = e.t.matmul(P[j][:, 0:n1 - n0],
                                       qsT2[rb:rb + 64, qq * 128:(qq + 1) * 128],
                                       kT2[rb:rb + 64, n0:n1],
                                       start=True, stop=True)
                j0 = qq // 4
                off = (qq % 4) * 128
                e.v.wait(h_mmA)
                e.v.tensor_tensor(s2[:, 0:128], P[j0][:, off:off + 128],
                                  maskaddQ[:, :], ALU.add)
                npart = 0
                for j in range(j0):
                    e.v.reduce_max(m_sc[:, npart:npart + 1], P[j][:, :],
                                   axis=AX.X, negate=True)
                    npart += 1
                if off > 0:
                    e.v.reduce_max(m_sc[:, npart:npart + 1], P[j0][:, 0:off],
                                   axis=AX.X, negate=True)
                    npart += 1
                e.v.reduce_max(m_sc[:, npart:npart + 1], s2[:, 0:128],
                               axis=AX.X, negate=True)
                npart += 1
                h_nm = e.v.tensor_reduce(nm[:, :], m_sc[:, 0:npart], axis=AX.X,
                                         op=ALU.min)
                e.t.wait(h_nm)
                htr = e.t.transpose(P[4][0:1, 0:128], nm[:, :], identf[:, :])
                e.s.wait(htr)
                h_mq = e.s.copy(mq_row[:, qq * 128:(qq + 1) * 128], P[4][0:1, 0:128])
                e.t.wait(h_mq)
            e.v.wait(h_mq)
            e.v.tensor_copy(kaug[0:64, :], kT2[rb:rb + 64, :])
            e.v.memset(kaug[64:65, :], 1.0)
            e.v.tensor_copy(qaug[0:64, :], qsT2[rb:rb + 64, :])
            with alp(reason="softmax shift in bf16 (exact: shift-invariant)"):
                h_aug = e.v.tensor_copy(qaug[64:65, :], mq_row[:, :])
            for kt in range(NT):
                with alp(reason="bf16 matmul operands"):
                    h_va = e.v.tensor_copy(
                        vaug[:, kt * 65: kt * 65 + 64],
                        xres[:, kt * 1024 + HD * h: kt * 1024 + HD * h + 64])
            for qc in range(4):
                ktmax = 4 * qc + 4
                h_ev = {}
                h_c = {}
                for kt in range(ktmax):
                    e.t.waits(h_aug, h_va)
                    hmm = e.t.matmul(P[4 + kt % 2][:, :],
                                     kaug[0:65, kt * 128:(kt + 1) * 128],
                                     qaug[0:65, qc * 512:(qc + 1) * 512],
                                     start=True, stop=True)
                    e.s.wait(hmm)
                    dst = ewt[:, (kt % 2) * 512:(kt % 2 + 1) * 512]
                    with alp(reason="bf16 matmul operands"):
                        if kt >= 4 * qc:
                            if h_exS_reader is not None:
                                e.s.wait(h_exS_reader)
                            hx = e.s.activation(exS[:, :], P[4 + kt % 2][:, :], AF.Exp,
                                                bias=0.0, scale=1.0)
                            if kt >= 2:
                                e.v.wait(h_c[kt - 2])
                            e.v.wait(hx)
                            off_i = kt - 4 * qc
                            h_ev[kt] = e.v.tensor_tensor(
                                dst, exS[:, :],
                                mask_ns[off_i], ALU.mult)
                            h_exS_reader = h_ev[kt]
                        else:
                            if kt >= 2:
                                e.s.wait(h_c[kt - 2])
                            h_ev[kt] = e.s.activation(dst, P[4 + kt % 2][:, :], AF.Exp,
                                                      bias=0.0, scale=1.0)
                    e.t.wait(h_ev[kt])
                    h_av = h_c[kt] = e.t.matmul(
                        P[6][0:65, :], vaug[:, kt * 65:(kt + 1) * 65], dst,
                        start=(kt == 0), stop=(kt == ktmax - 1))
                e.v.wait(h_av)
                h_rs = e.v.reciprocal(rsum[:, :], P[6][64:65, :])
                e.t.wait(h_rs)
                h_rep = e.t.matmul(P[3][0:64, :], p.onesf[0:1, 0:64], rsum[:, :],
                                   start=True, stop=True)
                e.s.wait(h_rep)
                h_rr = e.s.copy(rsrep[:, :], P[3][0:64, :])
                e.v.wait(h_rr)
                with alp(reason="bf16 matmul operands"):
                    h_yT = e.v.tensor_tensor(yT_h[:, qc * 512:(qc + 1) * 512],
                                             P[6][0:64, :], rsrep[:, :], ALU.mult)
                e.t.wait(h_yT)
            with alp(reason="bf16"):
                e.v.tensor_copy(qaug[0:64, :], xT_head_rows(xT, h))
                e.v.tensor_sub(kaug[0:64, 0:T - 1], qaug[0:64, 1:T],
                               yT_h[:, 0:T - 1])
            h_e = e.v.memset(kaug[0:64, T - 1:T], 0.0)
            e.t.wait(h_e)
            h_en = []
            for kt in range(NT):
                if kt >= 1:
                    e.t.wait(h_en[kt - 1])
                pe_sl = psE[:, (kt % 2) * 64:(kt % 2) * 64 + 64]
                ht = e.t.transpose(pe_sl, kaug[0:64, kt * 128:(kt + 1) * 128],
                                   p.identb[0:64, 0:64])
                e.v.wait(ht)
                h_en.append(e.v.tensor_copy(e_nat[:, kt * 64:(kt + 1) * 64], pe_sl))
            for qc in range(4):
                ktmax = 4 * qc + 4
                h_ev = {}
                h_c = {}
                for kt in range(ktmax):
                    hmm = e.t.matmul(P[4 + kt % 2][:, :],
                                     kaT2[rb:rb + 64, kt * 128:(kt + 1) * 128],
                                     qaT2[rb:rb + 64, qc * 512:(qc + 1) * 512],
                                     start=True, stop=True)
                    dst = ewt[:, (kt % 2) * 512:(kt % 2 + 1) * 512]
                    with alp(reason="bf16 matmul operands"):
                        if kt >= 4 * qc:
                            e.v.wait(hmm)
                            if kt >= 2:
                                e.v.wait(h_c[kt - 2])
                            off_i = kt - 4 * qc
                            h_ev[kt] = e.v.tensor_tensor(
                                dst, P[4 + kt % 2][:, :],
                                mask_st[off_i], ALU.mult)
                        else:
                            e.s.wait(hmm)
                            if kt >= 2:
                                e.s.wait(h_c[kt - 2])
                            h_ev[kt] = e.s.copy(dst, P[4 + kt % 2][:, :])
                    e.t.waits(h_ev[kt], h_en[-1])
                    h_y2 = h_c[kt] = e.t.matmul(
                        P[6][0:64, :], e_nat[:, kt * 64:(kt + 1) * 64], dst,
                        start=(kt == 0), stop=(kt == ktmax - 1))
                e.v.wait(h_y2)
                ytile = h // 2
                yrow = 64 * (h % 2)
                with alp(reason="bf16 matmul operands"):
                    h_acc = e.v.tensor_tensor(
                        yTsum[yrow:yrow + 64,
                              ytile * T + qc * 512: ytile * T + (qc + 1) * 512],
                        P[6][0:64, :], yT_h[:, qc * 512:(qc + 1) * 512], ALU.add)
                e.t.wait(h_acc)
            if io.get("dbg") is not None and h == 0:
                d = io["dbg"]
                e.gq.waits(e.v.milestone(), e.s.milestone())
                e.gq.dma(d["yT_h"], yT_h[:, :])
                e.gq.dma(d["e_nat"], e_nat[:, :])
                e.gq.dma(d["mq_row"], mq_row[:, :])
                e.gq.dma(d["qsT2"], qsT2[:, :])
                e.gq.dma(d["kT2"], kT2[:, :])
                e.gq.dma(d["kaT2"], kaT2[:, :])
                e.gq.dma(d["qaT2"], qaT2[:, :])
                e.gq.dma(d["ysum0"], yTsum[:, 0:T])

    wpt = wp_h.rearrange("(n p) c -> n p c", p=128)
    e.q.wait(e.v.milestone())
    for tcg in range(8):
        h_pr = None
        for k in range(NC):
            if h_prev_cast is not None:
                e.q.wait(h_prev_cast)
            hw = e.q.dma(wstage[:, :], wpt[k])
            if h_pr is not None:
                e.v.wait(h_pr)
            e.v.wait(hw)
            with alp(reason="bf16 matmul operands"):
                h_prev_cast = h_wp = e.v.tensor_copy(wp_bf[:, :], wstage[:, :])
            e.t.wait(h_wp)
            for tt in range(2):
                tc = tcg * 2 + tt
                for hh in range(2):
                    h_pr = e.t.matmul(P[2 * tt + hh][:, :],
                                      yTsum[:, k * T + tc * 128:k * T + (tc + 1) * 128],
                                      wp_bf[:, hh * 512:(hh + 1) * 512],
                                      start=(k == 0), stop=(k == NC - 1))
        e.v.wait(h_pr)
        for tt in range(2):
            tc = tcg * 2 + tt
            for hh in range(2):
                e.v.tensor_tensor(s2[:, hh * 512:(hh + 1) * 512], P[2 * tt + hh][:, :],
                                  xres[:, tc * 1024 + hh * 512:tc * 1024 + (hh + 1) * 512],
                                  ALU.add)
            h_out = ln.emit(e, s2[:, :], xres[:, tc * 1024:(tc + 1) * 1024])
        e.t.wait(h_out)
    io["done"] = h_out
    return io


def xT_head_rows(xT, h):
    tile = h // 2
    rb = 64 * (h % 2)
    return xT[rb:rb + 64, tile * T:(tile + 1) * T]


emit_l0attn.CONSTS = (SCALE, 0.5 * MA_K * SCALE)
emit_l1ffn.CONSTS = ()


ROUTE_SCALE = 2.5
NEXP = 16


def exS_f32(p):
    if not hasattr(p, "_exs_f32"):
        p._exs_f32 = p.sb("exsf32", [128, 8], F32)
    return p._exs_f32


def emit_l0ffn(p, e, io):
    nc = p.nc
    io["comb_sp"] = p.dram_out(f"comb_sp_{p.uid()}", [T, 128], BF16)
    xres = io["xres"]
    h_x = io["x_ready"]
    x_hbm = io["x_hbm"]
    alp = nc.allow_low_precision

    xT = p.sb("mxT", [128, NC * T], BF16)
    wstage = p.sb("mwstage", [128, 1024], F32)
    wstage2 = p.sb("mwstage2", [128, 1024], F32)
    wg_bf = p.sb("mwg", [128, NC * 128], BF16)
    wu_bf = p.sb("mwu", [128, NC * 128], BF16)
    wg_bf2 = p.sb("mwg2", [128, NC * 128], BF16)
    wu_bf2 = p.sb("mwu2", [128, NC * 128], BF16)
    w2_bf = p.sb("mw2", [128, 4 * 1024], BF16)
    gw_bf = p.sb("mgw", [128, NC * 16], BF16)
    gb_rep = p.sb("mgb", [128, 16], F32)
    gb_row = p.sb("mgbrow", [1, 16], F32)
    scores = p.sb("mscores", [128, 16], F32)
    biased = p.sb("mbiased", [128, 16], F32)
    work = p.sb("mwork", [128, 16], F32)
    mx = p.sb("mmx", [128, 1], F32)
    msk1 = p.sb("mmsk", [128, 16], F32)
    t4 = p.sb("mt4", [128, 1], F32)
    wsel = p.sb("mwsel", [128, 16], F32)
    rsum = p.sb("mrsum", [128, 1], F32)
    comb = p.sb("mcomb", [128, NT * 16], F32)
    combTbig = p.sb("mcombT", [128, T], BF16)
    combT = combTbig[0:16, :]
    combbf = p.sb("mcombbf", [128, 256], BF16)
    combrep = p.sb("mcombrep", [128, T], BF16)
    Rsel = p.sb("mRsel", [16, 16 * 128], BF16)
    scT = p.sb("mscT", [128, 4 * T], BF16)
    gsb = p.sb("mgsb", [128, 512], BF16)
    usb = p.sb("musb", [128, 512], BF16)
    ssb = p.sb("mssb", [128, 512], BF16)
    xtmp = p.sb("mxtmp", [128, 1024], F32)
    s2 = p.sb("ms2", [128, 1024], F32)
    ln = LNKit(p)

    P = [p.ps(f"mpool{i}", [128, 512], F32) for i in range(8)]

    e.g.memset(Rsel[:, :], 1.0)
    h_rsel = e.g.affine_select(Rsel[:, :], Rsel[:, :], [[-1, 16], [0, 128]],
                               ALU.is_equal, 0.0, base=0, channel_multiplier=1)

    identf = p.sb("midentf", [128, 128], F32)
    e.g.memset(identf[:, :], 1.0)
    h_idf = e.g.affine_select(identf[:, :], identf[:, :], [[-1, 128]], ALU.is_equal,
                              0.0, base=0, channel_multiplier=1)
    e.t.waits(h_x, io["consts"], h_idf, h_rsel)
    e.v.waits(h_idf, h_rsel)
    e.s.wait(h_idf)
    if "xT_h" in io:
        xTt = io["xT_h"].rearrange("(n p) t -> n p t", p=128)
        h_xT = None
        for j in range(NC):
            for half in range(2):
                hw = e.q.dma(wstage[:, :], xTt[j][:, half * 1024:(half + 1) * 1024])
                e.v.wait(hw)
                with nc.allow_low_precision(reason="bf16 matmul operands"):
                    h_xT = e.v.tensor_copy(
                        xT[:, j * T + half * 1024: j * T + (half + 1) * 1024],
                        wstage[:, :])
                e.q.wait(h_xT)
    else:
        h_ev = []
        ntp = 0
        for tc in range(NT):
            for j in range(NC):
                if ntp >= 2:
                    e.t.wait(h_ev[ntp - 2])
                ht = e.t.transpose(P[ntp % 2][:, 0:128],
                                   xres[:, tc * 1024 + j * 128: tc * 1024 + (j + 1) * 128],
                                   identf[:, :])
                e.s.wait(ht)
                with nc.allow_low_precision(reason="bf16 matmul operands"):
                    h_ev.append(e.s.copy(xT[:, j * T + tc * 128: j * T + (tc + 1) * 128],
                                         P[ntp % 2][:, 0:128]))
                ntp += 1
        h_xT = h_ev[-1]

    if io.get("dbg") is not None:
        e.gq.waits(h_xT)
        e.gq.dma(io["dbg"]["xT1"], xT[:, 0:2048])
        e.gq.dma(io["dbg"]["xT1b"], xT[:, 7 * T:8 * T])
    h_gw = e.q.dma(wstage[:, 0:128].rearrange("p (n c) -> p n c", c=16),
                   io["gate_wT_h"].rearrange("(n p) c -> p n c", p=128))
    e.v.wait(h_gw)
    with alp(reason="bf16 matmul operands"):
        h_gwc = e.v.tensor_copy(gw_bf[:, :], wstage[:, 0:128])
    h_gb = e.q.dma(gb_row[:, :], io["gate_b_h"][:, :])
    e.t.waits(h_gb, io["consts"])
    h_gbm = e.t.matmul(P[2][0:128, 0:16], p.onesf[:, :], gb_row[:, :],
                       start=True, stop=True)
    e.s.wait(h_gbm)
    h_gbr = e.s.copy(gb_rep[:, :], P[2][:, 0:16])

    if io.get("dbg") is not None:
        io["_dbg_sc"] = p.sb("dbgsc", [128, 256], F32)
        io["_dbg_bi"] = p.sb("dbgbi", [128, 256], F32)
        io["_dbg_ws"] = p.sb("dbgws", [128, 256], F32)
        io["_dbg_rs"] = p.sb("dbgrs", [128, 16], F32)

    e.g.wait(h_xT)
    h_z = None
    for tc in range(NT):
        h_z = e.g.memset(xres[:, tc * 1024:(tc + 1) * 1024], 0.0)
    e.t.wait(h_z)
    scraw = p.sb("mscraw", [128, NT * 16], F32)
    scraw2 = p.sb("mscraw2", [128, NT * 16], F32)
    e.t.waits(h_xT, h_gwc)
    h_tnl = []
    for tc in range(NT):
        for k in range(NC):
            h_rt = e.t.matmul(P[3][:, 0:16], xT[:, k * T + tc * 128:k * T + (tc + 1) * 128],
                              gw_bf[:, k * 16:(k + 1) * 16],
                              start=(k == 0), stop=(k == NC - 1))
        e.s.wait(h_rt)
        h_tnl.append(e.s.activation(scraw[:, tc * 16:(tc + 1) * 16], P[3][:, 0:16],
                                    AF.Tanh, bias=0.0, scale=p.cval(0.5)))
        e.t.wait(h_tnl[-1])
    io["_route2"] = (scraw, scraw2, h_tnl[-1])
    for tc in range(NT if False else 0):
        pass
    def route_stage2(tc):
        e.v.tensor_scalar(scores[:, :], scraw2[:, tc * 16:(tc + 1) * 16], 0.5, 0.5,
                          ALU.mult, ALU.add)
        h_sc = e.v.tensor_tensor(biased[:, :], scores[:, :], gb_rep[:, :], ALU.add)
        e.v.tensor_copy(work[:, :], biased[:, :])
        for r in range(4):
            e.v.reduce_max(t4[:, :] if r == 3 else mx[:, :], work[:, :], axis=AX.X)
            if r < 3:
                e.v.tensor_tensor(msk1[:, :], work[:, :],
                                  mx[:, :].broadcast_to([128, 16]), ALU.is_ge)
                e.v.scalar_tensor_tensor(work[:, :], msk1[:, :], -1e9, work[:, :],
                                         ALU.mult, ALU.add)
        e.v.tensor_tensor(msk1[:, :], biased[:, :],
                          t4[:, :].broadcast_to([128, 16]), ALU.is_ge)
        e.v.tensor_tensor(wsel[:, :], scores[:, :], msk1[:, :], ALU.mult)
        e.v.reduce_sum(rsum[:, :], wsel[:, :], axis=AX.X)
        e.v.reciprocal(mx[:, :], rsum[:, :])
        e.v.tensor_tensor(work[:, :], wsel[:, :],
                          mx[:, :].broadcast_to([128, 16]), ALU.mult)
        e.v.tensor_scalar(comb[:, tc * 16:(tc + 1) * 16], work[:, :],
                          ROUTE_SCALE, None, ALU.mult)
        if io.get("dbg") is not None:
            e.v.tensor_copy(io["_dbg_sc"][:, tc * 16:(tc + 1) * 16], scores[:, :])
            e.v.tensor_copy(io["_dbg_bi"][:, tc * 16:(tc + 1) * 16], biased[:, :])
            e.v.tensor_copy(io["_dbg_ws"][:, tc * 16:(tc + 1) * 16], wsel[:, :])
            e.v.tensor_copy(io["_dbg_rs"][:, tc:tc + 1], rsum[:, :])
    h_cT = []

    def emit_stage2_and_combT():
        e.v.wait(h_tnl[-1])
        e.v.tensor_copy(scraw2[:, :], scraw[:, :])
        for tc in range(NT):
            route_stage2(tc)
        with alp(reason="bf16 matmul operands"):
            h_cb = e.v.tensor_copy(combbf[:, :], comb[:, :])
        e.gq.wait(h_cb)
        hsp = None
        for tc in range(NT):
            hsp = e.gq.dma(io["comb_sp"].rearrange("(n p) e -> n p e", p=128)[tc][:, 0:16],
                           combbf[:, tc * 16:(tc + 1) * 16])
        e.q.wait(hsp)
        inst = nc.sync.dma_start_transpose(combTbig[:, :], io["comb_sp"][:, :])
        inst.then_inc(e.q.sem, 16)
        e.q.n += 16
        h_cT.append((e.q.sem, e.q.n))
        e.v.wait(h_cT[-1])

    w13T_h = io["w13T_h"]
    w2T_h = io["w2T_h"]
    w13sT_h = io["w13_sT_h"]
    w2sT_h = io["w2_sT_h"]
    h_wst_reader = h_gwc
    h_acc = {}
    h_sct = None
    for idx, ei in enumerate([NEXP] + list(range(NEXP))):
        shared = ei == NEXP
        if idx == 1:
            emit_stage2_and_combT()
        if io.get("BISECT_CONST_COMBREP") and not shared:
            e.t.waits(h_cT[-1], h_z)
            h_crep = e.v.memset(combrep[:, :], 0.1)
        elif not shared:
            e.t.waits(h_cT[-1], h_z)
            h_crm = e.t.matmul(P[2][:, 0:T // 4], Rsel[:, ei * 128:(ei + 1) * 128],
                               combT[:, 0:512], start=True, stop=True)
            hs = []
            for quarter in range(4):
                if quarter > 0:
                    h_crm = e.t.matmul(P[2][:, 0:512], Rsel[:, ei * 128:(ei + 1) * 128],
                                       combT[:, quarter * 512:(quarter + 1) * 512],
                                       start=True, stop=True)
                e.v.wait(h_crm)
                with alp(reason="bf16"):
                    hs.append(e.v.tensor_copy(combrep[:, quarter * 512:(quarter + 1) * 512],
                                              P[2][:, 0:512]))
                e.t.wait(hs[-1])
            h_crep = hs[-1]
        stg = (wstage, wstage2)
        wgb = (wg_bf, wg_bf2)
        wub = (wu_bf, wu_bf2)
        h_dma = {}
        h_cast = {}

        def w13_src(fc, half):
            col = (half * 512) + fc * 128
            hsrc = w13sT_h if shared else w13T_h[ei]
            return hsrc[:, col:col + 128].rearrange("(n p) c -> p n c", p=128)

        for pre in range(2):
            e.q.wait(h_cast.get((pre - 2, "g"), h_wst_reader))
            h_dma[(0, pre)] = e.q.dma(
                stg[pre][:, :].rearrange("p (n c) -> p n c", c=128), w13_src(0, pre))
        for fc in range(4):
            sl = fc % 2
            for half in range(2):
                e.v.wait(h_dma[(fc, half)])
                with alp(reason="bf16 matmul operands"):
                    h_cast[(fc, half)] = e.v.tensor_copy(
                        (wgb if half == 0 else wub)[sl][:, :], stg[half][:, :])
                h_wst_reader = h_cast[(fc, half)]
                if fc + 1 < 4:
                    e.q.wait(h_cast[(fc, half)])
                    h_dma[(fc + 1, half)] = e.q.dma(
                        stg[half][:, :].rearrange("p (n c) -> p n c", c=128),
                        w13_src(fc + 1, half))
            e.t.wait(h_wst_reader)
            for tq in range(4):
                pg = P[4 + tq % 2]
                pu = P[6 + tq % 2]
                for k in range(NC):
                    e.t.matmul(pg[:, :], wgb[sl][:, k * 128:(k + 1) * 128],
                               xT[:, k * T + tq * 512:k * T + (tq + 1) * 512],
                               start=(k == 0), stop=(k == NC - 1))
                for k in range(NC):
                    h_gu = e.t.matmul(pu[:, :], wub[sl][:, k * 128:(k + 1) * 128],
                                      xT[:, k * T + tq * 512:k * T + (tq + 1) * 512],
                                      start=(k == 0), stop=(k == NC - 1))
                e.s.wait(h_gu)
                if h_sct is not None:
                    e.s.wait(h_sct)
                with alp(reason="bf16 matmul operands"):
                    h_si = e.s.activation(gsb[:, :], pg[:, :], AF.Silu, bias=0.0, scale=1.0)
                    h_uc = e.s.copy(usb[:, :], pu[:, :])
                e.v.waits(h_si, h_uc)
                if not shared:
                    e.v.wait(h_crep)
                dst = scT[:, fc * T + tq * 512: fc * T + (tq + 1) * 512]
                with alp(reason="bf16 matmul operands"):
                    e.v.tensor_tensor(ssb[:, :], gsb[:, :], usb[:, :], ALU.mult)
                    if shared:
                        h_sct = e.v.tensor_copy(dst, ssb[:, :])
                    else:
                        h_sct = e.v.tensor_tensor(
                            dst, ssb[:, :], combrep[:, tq * 512:(tq + 1) * 512], ALU.mult)
                e.t.wait(h_sct)
        def w2_src(fc):
            return (w2sT_h if shared else w2T_h[ei])[fc * 128:(fc + 1) * 128, :]

        h_w2d = {}
        for pre in range(2):
            e.q.wait(h_cast[(3, pre)])
            h_w2d[pre] = e.q.dma(stg[pre][:, :], w2_src(pre))
        for fc in range(4):
            e.v.wait(h_w2d[fc])
            with alp(reason="bf16 matmul operands"):
                h_wst_reader = e.v.tensor_copy(w2_bf[:, fc * 1024:(fc + 1) * 1024],
                                               stg[fc % 2][:, :])
            if fc + 2 < 4:
                e.q.wait(h_wst_reader)
                h_w2d[fc + 2] = e.q.dma(stg[fc % 2][:, :], w2_src(fc + 2))
        e.t.wait(h_wst_reader)
        if io.get("dbg") is not None and ei == 0:
            e.gq.waits(e.v.milestone(), e.s.milestone())
            e.gq.dma(io["dbg"]["scT0"], scT[:, :])
            e.gq.dma(io["dbg"]["combrep0"], combrep[:, :])
        for tc in range(NT):
            pa, pb_ = P[4 + 2 * (tc % 2)], P[5 + 2 * (tc % 2)]
            if tc >= 2:
                e.t.wait(h_acc[tc - 2])
            elif idx > 0:
                e.t.wait(h_acc[NT - 2 + tc])
            for fc in range(4):
                for hh in range(2):
                    h_ym = e.t.matmul((pa if hh == 0 else pb_)[:, :],
                                      scT[:, fc * T + tc * 128: fc * T + (tc + 1) * 128],
                                      w2_bf[:, fc * 1024 + hh * 512: fc * 1024 + (hh + 1) * 512],
                                      start=(fc == 0), stop=(fc == 3))
            e.v.wait(h_ym)
            with alp(reason="moe accumulate"):
                e.v.tensor_tensor(xres[:, tc * 1024:tc * 1024 + 512],
                                  pa[:, :], xres[:, tc * 1024:tc * 1024 + 512], ALU.add)
                h_acc[tc] = e.v.tensor_tensor(
                    xres[:, tc * 1024 + 512:(tc + 1) * 1024],
                    pb_[:, :], xres[:, tc * 1024 + 512:(tc + 1) * 1024], ALU.add)
    if io.get("dbg") is not None:
        d = io["dbg"]
        e.gq.waits(e.v.milestone(), e.s.milestone())
        e.gq.dma(d["scall"], io["_dbg_sc"][:, :])
        e.gq.dma(d["biall"], io["_dbg_bi"][:, :])
        e.gq.dma(d["wsall"], io["_dbg_ws"][:, :])
        e.gq.dma(d["rsall"], io["_dbg_rs"][:, :])
        e.gq.dma(d["cpool"], p._ctile[:, :])
        e.gq.dma(d["scraw"], scraw[:, :])
        e.gq.dma(d["scraw2"], scraw2[:, :])
        e.gq.dma(d["comb"], comb[:, :])
        e.gq.dma(d["combT"], combT[:, :])
        e.gq.dma(io["dbg"]["yacc"], xres[:, :])
    xt = x_hbm.rearrange("(n p) c -> n p c", p=128)
    h_out = None
    for tc in range(NT):
        hx = e.q.dma(xtmp[:, :], xt[tc])
        e.v.waits(hx, h_acc.get(NT - 2), h_acc.get(NT - 1))
        e.v.tensor_tensor(s2[:, :], xres[:, tc * 1024:(tc + 1) * 1024], xtmp[:, :],
                          ALU.add)
        h_out = ln.emit(e, s2[:, :], xres[:, tc * 1024:(tc + 1) * 1024])
        e.q.wait(h_out)
    io["done"] = h_out
    return io


emit_l0ffn.CONSTS = (0.5,)


DIN = 2048
DST = 16
DTR = 64


def emit_l1attn(p, e, io):
    nc = p.nc
    xres = io["xres"]
    h_x = io["x_ready"]
    x_hbm = io["x_hbm"]
    alp = nc.allow_low_precision

    bufB = p.sb("abufB", [128, 16 * T], BF16)
    repc = xres[:, :].bitcast(BF16)
    dbcT = p.sb("adbcT", [128, T], F32)
    dltT = p.sb("adltT", [64, T], BF16)
    BCbf = p.sb("aBC", [32, T], BF16)
    Rsel32 = p.sb("aRsel", [32, 32 * 128], BF16)
    wxp = p.sb("awxp", [128, 16 * 96], BF16)
    wdt = p.sb("awdt", [64, T], BF16)
    acols = p.sb("aacols", [128, 16 * 16], F32)
    dcols = p.sb("adcols", [128, 16], F32)
    bdtcols = p.sb("abdt", [128, 16], F32)
    wconvc = p.sb("awconv", [128, 64], F32)
    wstage = p.sb("awstage", [128, 1024], F32)
    wchunk = p.sb("awchunk", [128, 1024], BF16)
    convsrc = p.sb("aconvsrc", [128, T + 4], BF16)
    acc1 = p.sb("aacc1", [128, T], BF16)
    acc2 = p.sb("aacc2", [128, T], BF16)
    zst = p.sb("azst", [128, T], BF16)
    xcst = p.sb("axcst", [128, T], BF16)
    xch = p.sb("axch", [128, 1024], BF16)
    zsh = p.sb("azsh", [128, 1024], BF16)
    uTd = p.sb("auTd", [128, 1024], BF16)
    dA = p.sb("adA", [128, 1024], BF16)
    dAb = p.sb("adAb", [128, 1024], BF16)
    dBx = p.sb("adBx", [128, 1024], BF16)
    hsc = p.sb("ahsc", [128, 1024], BF16)
    hC = p.sb("ahC", [128, 1024], BF16)
    hlast = p.sb("ahlast", [128, 256], F32)
    s2 = dbcT[:, 0:1024]
    epi4 = dbcT[:, 1024:2048]
    identf = p.sb("aidentf", [128, 128], F32)
    ln = LNKit(p)

    PA = [p.ps(f"apsA{i}", [128, 512], F32) for i in range(2)]
    PB = p.ps("apsB", [128, T], F32)
    PC = p.ps("apsC", [128, 1024], F32)

    xc_sp = p.dram_out(f"xc_sp_{p.uid()}", [16, 128, T], BF16)
    zs_sp = p.dram_out(f"zs_sp_{p.uid()}", [16, 128, T], BF16)

    e.g.memset(identf[:, :], 1.0)
    e.g.affine_select(identf[:, :], identf[:, :], [[-1, 128]], ALU.is_equal, 0.0,
                      base=0, channel_multiplier=1)
    e.g.memset(Rsel32[:, :], 1.0)
    h_gp = e.g.affine_select(Rsel32[:, :], Rsel32[:, :], [[-1, 32], [0, 128]],
                             ALU.is_equal, 0.0, base=0, channel_multiplier=1)
    e.g.memset(convsrc[:, 0:4], 0.0)

    h_ac = e.q.dma(acols[:, :], io["A_log_cols_h"][:, :])
    h_dc = e.q.dma(dcols[:, :], io["D_cols_h"][:, :])
    h_bc = e.q.dma(bdtcols[:, :], io["bdt_cols_h"][:, :])
    h_wc = e.q.dma(wconvc[:, :], io["Wconv_cols_h"][:, :])
    for g in range(2):
        h_wx = e.q.dma(wstage[:, 0:768].rearrange("p (n c) -> p n c", c=96),
                       io["W_xprojT_h"][g * 1024:(g + 1) * 1024, :]
                       .rearrange("(n p) c -> p n c", p=128))
        e.v.waits(h_wx, io["consts"])
        with alp(reason="bf16 matmul operands"):
            h_wxc = e.v.tensor_copy(wxp[:, g * 768:(g + 1) * 768], wstage[:, 0:768])
        e.q.wait(h_wxc)
    for half in range(2):
        h_wd = e.q.dma(wstage[0:64, :], io["W_dtT_h"][:, half * 1024:(half + 1) * 1024])
        e.v.wait(h_wd)
        with alp(reason="bf16 matmul operands"):
            h_wdc = e.v.tensor_copy(wdt[:, half * 1024:(half + 1) * 1024],
                                    wstage[0:64, :])
        e.q.wait(h_wdc)
    e.s.waits(h_ac, io["consts"])
    aneg = acc1[:, :].bitcast(F32)
    h_a1 = e.s.activation(aneg[:, 0:256], acols[:, :], AF.Exp, bias=0.0, scale=1.0)
    e.v.wait(h_a1)
    h_a2 = e.v.tensor_scalar(acols[:, :], aneg[:, 0:256], -1.0, None, ALU.mult)

    xT = bufB[:, 0:NC * T]
    e.t.waits(h_x, io["consts"], h_gp)
    e.v.wait(h_gp)
    e.s.wait(h_gp)
    h_xT_ev = []
    ntp = 0
    for tc in range(NT):
        for j in range(NC):
            if ntp >= 2:
                e.t.wait(h_xT_ev[ntp - 2])
            ht = e.t.transpose(PA[ntp % 2][:, 0:128],
                               xres[:, tc * 1024 + j * 128: tc * 1024 + (j + 1) * 128],
                               identf[:, :])
            e.s.wait(ht)
            with alp(reason="bf16 matmul operands"):
                h_xT_ev.append(e.s.copy(xT[:, j * T + tc * 128: j * T + (tc + 1) * 128],
                                        PA[ntp % 2][:, 0:128]))
            ntp += 1
    h_xT = h_xT_ev[-1]

    winT = io["W_inT_h"]
    h_wst_reader = h_wdc
    h_dbc_mm = None
    h_spill = {}
    for fi in range(32):
        is_z = fi >= 16
        d = fi - 16 if is_z else fi
        e.q.wait(h_wst_reader)
        src = winT[:, fi * 128:(fi + 1) * 128].rearrange("(n p) c -> p n c", p=128)
        hw = e.q.dma(wstage[:, :].rearrange("p (n c) -> p n c", c=128), src)
        e.v.wait(hw)
        with alp(reason="bf16 matmul operands"):
            h_wst_reader = e.v.tensor_copy(wchunk[:, :], wstage[:, :])
        e.t.waits(h_wst_reader, h_xT)
        for tq in range(4):
            pbk = PA[tq % 2]
            for k in range(NC):
                hmm = e.t.matmul(pbk[:, :], wchunk[:, k * 128:(k + 1) * 128],
                                 xT[:, k * T + tq * 512:k * T + (tq + 1) * 512],
                                 start=(k == 0), stop=(k == NC - 1))
            e.s.wait(hmm)
            hsp = h_spill.get(fi - 1)
            if hsp is not None:
                e.s.wait(hsp)
            with alp(reason="bf16 matmul operands"):
                if is_z:
                    h_t1 = e.s.activation(acc1[:, 0:512], pbk[:, :], AF.Tanh,
                                          bias=0.0, scale=p.cval(0.5))
                    h_t2 = e.s.copy(acc2[:, 0:512], pbk[:, :])
                    e.v.waits(h_t1, h_t2)
                    e.v.tensor_scalar(acc1[:, 512:1024], acc1[:, 0:512], 0.5, 0.5,
                                      ALU.mult, ALU.add)
                    h_ev = e.v.tensor_tensor(zst[:, tq * 512:(tq + 1) * 512],
                                             acc2[:, 0:512], acc1[:, 512:1024], ALU.mult)
                else:
                    h_ev = e.s.copy(convsrc[:, 4 + tq * 512: 4 + (tq + 1) * 512],
                                    pbk[:, :])
            e.t.wait(h_ev)
        if is_z:
            e.q.wait(h_ev)
            h_spill[fi] = e.q.dma(zs_sp[d], zst[:, :])
        else:
            e.v.waits(h_ev, h_wc)
            with alp(reason="bf16"):
                e.v.tensor_scalar(acc1[:, :], convsrc[:, 1:1 + T],
                                  wconvc[:, 4 * d:4 * d + 1], None, ALU.mult)
                e.v.scalar_tensor_tensor(acc2[:, :], convsrc[:, 2:2 + T],
                                         wconvc[:, 4 * d + 1:4 * d + 2], acc1[:, :],
                                         ALU.mult, ALU.add)
                e.v.scalar_tensor_tensor(acc1[:, :], convsrc[:, 3:3 + T],
                                         wconvc[:, 4 * d + 2:4 * d + 3], acc2[:, :],
                                         ALU.mult, ALU.add)
                h_cv = e.v.scalar_tensor_tensor(acc2[:, :], convsrc[:, 4:4 + T],
                                                wconvc[:, 4 * d + 3:4 * d + 4], acc1[:, :],
                                                ALU.mult, ALU.add)
            e.s.wait(h_cv)
            hsp = h_spill.get(fi - 1)
            if hsp is not None:
                e.s.wait(hsp)
            if h_dbc_mm is not None:
                e.s.wait(h_dbc_mm)
                e.v.wait(h_dbc_mm)
            h_tn = e.s.activation(acc1[:, :], acc2[:, :], AF.Tanh, bias=0.0,
                                  scale=p.cval(0.5))
            e.v.wait(h_tn)
            with alp(reason="bf16"):
                e.v.tensor_scalar(acc1[:, :], acc1[:, :], 0.5, 0.5, ALU.mult, ALU.add)
                h_si = e.v.tensor_tensor(xcst[:, :], acc2[:, :], acc1[:, :], ALU.mult)
            e.t.waits(h_si, h_wxc)
            for nq in range(4):
                h_dbc_mm = e.t.matmul(PB[0:96, nq * 512:(nq + 1) * 512],
                                      wxp[:, d * 96:(d + 1) * 96],
                                      xcst[:, nq * 512:(nq + 1) * 512],
                                      start=(d == 0), stop=(d == 15))
            e.q.wait(h_si)
            h_spill[fi] = e.q.dma(xc_sp[d], xcst[:, :])
            if io.get("dbg") is not None and fi == 0:
                e.gq.waits(e.s.milestone(), h_spill[0])
                e.gq.dma(io["dbg"]["xcst0"], xcst[:, :])
                e.gq.dma(io["dbg"]["xcsp0"], xc_sp[0])

    e.v.wait(h_dbc_mm)
    h_dbc = None
    for nq in range(4):
        h_dbc = e.v.tensor_copy(dbcT[0:96, nq * 512:(nq + 1) * 512],
                                PB[0:96, nq * 512:(nq + 1) * 512])
    with alp(reason="bf16 matmul operands"):
        h_dlt = e.v.tensor_copy(dltT[:, :], dbcT[0:64, :])
        h_bcc = e.v.tensor_copy(BCbf[:, :], dbcT[64:96, :])

    h_dmm = {}
    for d in range(NT):
        e.t.waits(h_dlt, h_wdc)
        if d >= 1:
            e.t.wait(h_dmm[d - 1])
        for nq in range(4):
            hm = e.t.matmul(PB[:, nq * 512:(nq + 1) * 512], wdt[:, d * 128:(d + 1) * 128],
                            dltT[:, nq * 512:(nq + 1) * 512],
                            start=True, stop=True)
        e.v.waits(hm, h_bc)
        for hh in range(2):
            xin = s2
            e.v.tensor_scalar(xin, PB[:, hh * 1024:(hh + 1) * 1024],
                              bdtcols[:, d:d + 1], None, ALU.add)
            True
            m_ = acc1[:, :].bitcast(F32)
            arg = acc2[:, :].bitcast(F32)
            e.v.tensor_scalar(m_, xin, 0.0, None, ALU.max)
            h_arg = e.v.scalar_tensor_tensor(arg, m_, -2.0, xin, ALU.mult, ALU.add)
            e.s.wait(h_arg)
            h_ex = e.s.activation(xin, arg, AF.Exp, bias=0.0, scale=1.0)
            h_ln = e.s.activation(arg, xin, AF.Ln, bias=p.cval(1.0), scale=1.0)
            e.v.wait(h_ln)
            with alp(reason="bf16 delta"):
                h_dmm[d] = e.v.tensor_tensor(
                    bufB[:, d * T + hh * 1024: d * T + (hh + 1) * 1024], m_, arg, ALU.add)

    if io.get("dbg") is not None:
        d_ = io["dbg"]
        e.gq.waits(e.v.milestone(), e.s.milestone())
        e.gq.dma(d_["dbcT"], dbcT[:, :])
        e.gq.dma(d_["delta0"], bufB[:, 0:T])
        e.gq.dma(d_["delta7"], bufB[:, 7 * T:8 * T])
        e.gq.dma(d_["xc0"], xc_sp[0])
        e.gq.dma(d_["zs0"], zs_sp[0])
    h_out = None
    h_rep_reader = None
    h_hl_w = {}
    h_scan_prev = None
    h_accum_prev = None
    h_da = {}
    dabufs = None
    for th in range(2):
        t0 = th * 1024
        e.t.waits(h_bcc, e.v.milestone())
        h_repev = []
        for idx in range(32):
            if h_rep_reader is not None:
                e.t.wait(h_rep_reader)
            for hh in range(2):
                hr = e.t.matmul(PB[:, hh * 512:(hh + 1) * 512],
                                Rsel32[:, idx * 128:(idx + 1) * 128],
                                BCbf[:, t0 + hh * 512:t0 + (hh + 1) * 512],
                                start=True, stop=True)
            e.v.wait(hr)
            with alp(reason="bf16"):
                h_repev.append(e.v.tensor_copy(repc[:, idx * 1024:(idx + 1) * 1024],
                                               PB[:, 0:1024]))
            e.t.wait(h_repev[-1])
        h_rep = h_repev[-1]
        for d in range(NT):
            e.q.wait(h_spill[d])
            hx1 = e.q.dma(xch[:, :], xc_sp[d][:, t0:t0 + 1024])
            e.v.waits(hx1, h_dmm.get(NT - 1), h_rep)
            with alp(reason="bf16"):
                e.v.tensor_tensor(uTd[:, :], bufB[:, d * T + t0: d * T + t0 + 1024],
                                  xch[:, :], ALU.mult)
            dabufs = (dA, dAb)
            h_scan_n = {}
            if h_scan_prev is not None:
                e.s.wait(h_scan_prev)
            h_da[0] = e.s.activation(dabufs[0][:, :],
                                     bufB[:, d * T + t0: d * T + t0 + 1024],
                                     AF.Exp, bias=0.0,
                                     scale=acols[:, d * 16:d * 16 + 1])
            for n in range(16):
                if n + 1 < 16:
                    if h_scan_n.get(n - 1) is not None:
                        e.s.wait(h_scan_n[n - 1])
                    h_da[n + 1] = e.s.activation(
                        dabufs[(n + 1) % 2][:, :],
                        bufB[:, d * T + t0: d * T + t0 + 1024], AF.Exp, bias=0.0,
                        scale=acols[:, d * 16 + n + 1:d * 16 + n + 2])
                with alp(reason="bf16"):
                    h_db = e.v.tensor_tensor(dBx[:, :], uTd[:, :],
                                             repc[:, n * 1024:(n + 1) * 1024], ALU.mult)
                e.v.wait(h_da[n])
                ini = 0.0 if th == 0 else hlast[:, d * 16 + n:d * 16 + n + 1]
                with alp(reason="bf16 scan"):
                    h_h = e.v.tensor_tensor_scan(hsc[:, :], dabufs[n % 2][:, :],
                                                 dBx[:, :], ini, ALU.mult, ALU.add)
                h_scan_n[n] = h_h
                h_scan_prev = h_h
                if io.get("dbg") is not None and th == 0 and d == 0 and n == 15:
                    e.gq.wait(h_h)
                    e.gq.dma(io["dbg"]["dA00"], dA[:, :])
                    e.gq.dma(io["dbg"]["dBx00"], dBx[:, :])
                    e.gq.dma(io["dbg"]["h00"], hsc[:, :])
                    e.gq.dma(io["dbg"]["uTd0"], uTd[:, :])
                    e.gq.dma(io["dbg"]["brep0"], repc[:, 15 * 1024:16 * 1024])
                    e.gq.dma(io["dbg"]["acols"], acols[:, :])
                    hdmp = e.gq.dma(io["dbg"]["crep0"], repc[:, 31 * 1024:32 * 1024])
                    e.v.wait(hdmp)
                    e.s.wait(hdmp)
                if th == 0:
                    h_hl_w[(d, n)] = e.v.tensor_copy(hlast[:, d * 16 + n:d * 16 + n + 1],
                                                     hsc[:, 1023:1024])
                if h_accum_prev is not None:
                    e.v.wait(h_accum_prev)
                with alp(reason="bf16"):
                    h_hc = e.v.tensor_tensor(hC[:, :], hsc[:, :],
                                             repc[:, (16 + n) * 1024:(17 + n) * 1024],
                                             ALU.mult)
                e.t.wait(h_hc)
                for hh in range(2):
                    h_acc = e.t.matmul(PC[:, hh * 512:(hh + 1) * 512], p.identb[:, :],
                                       hC[:, hh * 512:(hh + 1) * 512],
                                       start=(n == 0), stop=(n == 15))
                h_accum_prev = h_acc
            e.q.wait(h_spill[16 + d])
            hz1 = e.q.dma(zsh[:, :], zs_sp[d][:, t0:t0 + 1024])
            e.v.waits(h_acc, hz1)
            e.v.scalar_tensor_tensor(s2, xch[:, :], dcols[:, d:d + 1], PC[:, :],
                                     ALU.mult, ALU.add)
            with alp(reason="bf16"):
                h_yf = e.v.tensor_tensor(bufB[:, d * T + t0: d * T + t0 + 1024],
                                         s2, zsh[:, :], ALU.mult)
            e.t.wait(h_yf)
            e.q.wait(h_yf)
            h_rep_reader = h_yf

    if io.get("dbg") is not None:
        e.gq.waits(e.v.milestone(), e.s.milestone())
        e.gq.dma(io["dbg"]["yf0"], bufB[:, 0:T])
        e.gq.dma(io["dbg"]["hlast"], hlast[:, :])
    wout_slots = [acc1[:, 0:1024], acc1[:, 1024:2048], acc2[:, 0:1024],
                  acc2[:, 1024:2048], zst[:, 0:1024], zst[:, 1024:2048],
                  xcst[:, 0:1024], xcst[:, 1024:2048], convsrc[:, 4:1028],
                  convsrc[:, 1028:2052], dA[:, :], dBx[:, :], hsc[:, :], hC[:, :],
                  uTd[:, :], wchunk[:, 0:1024]]
    woutT = io["W_outT_h"]
    h_wo = []
    for k in range(NT):
        e.q.wait(h_rep_reader if k == 0 else h_wo[-1])
        hw = e.q.dma(wstage[:, :], woutT.rearrange("(n p) c -> n p c", p=128)[k])
        e.v.wait(hw)
        with alp(reason="bf16 matmul operands"):
            h_wo.append(e.v.tensor_copy(wout_slots[k], wstage[:, :]))
    xt = x_hbm.rearrange("(n p) c -> n p c", p=128)
    e.t.wait(h_wo[-1])
    h_ep = None
    for tc in range(NT):
        if h_ep is not None:
            e.t.wait(h_ep)
        for k in range(NT):
            for hh in range(2):
                hm = e.t.matmul(PC[:, hh * 512:(hh + 1) * 512],
                                bufB[:, k * T + tc * 128:k * T + (tc + 1) * 128],
                                wout_slots[k][:, hh * 512:(hh + 1) * 512],
                                start=(k == 0), stop=(k == NT - 1))
        hx2 = e.q.dma(epi4, xt[tc])
        e.v.waits(hm, hx2)
        e.v.tensor_tensor(s2, PC[:, :], epi4, ALU.add)
        h_out = ln.emit(e, s2, xres[:, tc * 1024:(tc + 1) * 1024])
        h_ep = h_out
        e.q.wait(h_out)
    io["done"] = h_out
    return io


emit_l1attn.CONSTS = (1.0, 0.5)


def fused_program():
    p = Prog()
    x_h = p.dram_in("x", [T, C], F32)
    y_h = p.dram_out("y", [T, C], F32)
    w = {}
    for name, shape in (("W_attn_h", [1024, 2048]), ("W_k2_h", [1024, 1024]),
                        ("W_proj_h", [1024, 1024]), ("gate_wT_h", [1024, 16]),
                        ("gate_b_h", [1, 16]), ("w13T_h", [16, 1024, 1024]),
                        ("w2T_h", [16, 512, 1024]), ("w13_sT_h", [1024, 1024]),
                        ("w2_sT_h", [512, 1024]), ("W_inT_h", [1024, 4096]),
                        ("W_xprojT_h", [2048, 96]), ("W_dtT_h", [64, 2048]),
                        ("W_outT_h", [2048, 1024]), ("A_log_cols_h", [128, 256]),
                        ("D_cols_h", [128, 16]), ("bdt_cols_h", [128, 16]),
                        ("Wconv_cols_h", [128, 64]), ("weight_h", [1024, 1024]),
                        ("bias_h", [1, 1024])):
        w[name] = p.dram_in(name[:-2], shape)
    scr0 = p.dram_out("xspill0", [T, C], F32)
    scr1 = p.dram_out("xspill1", [T, C], F32)
    xres = p.sb("xres", [128, NT * 1024], F32)
    p.prep_consts(LN_EPS, -0.5, SCALE, 0.5 * MA_K * SCALE, 0.5, 1.0,
                  ident=True, ones_f32=True)

    def body(e):
        consts = p.emit_consts(e)
        allw = (e.v, e.s, e.t, e.g, e.q, e.gq, e.sq)
        for wr in (e.v, e.s, e.t):
            wr.wait(consts)

        def full_barrier():
            for a in allw:
                for b in allw:
                    if b.n > 0 and b.sem is not a.sem:
                        a.wait((b.sem, b.n))

        xt = x_h.rearrange("(n p) c -> n p c", p=128)
        h = None
        for tc in range(NT):
            h = e.q.dma(xres[:, tc * 1024:(tc + 1) * 1024], xt[tc])

        io0 = {"xres": xres, "x_ready": h, "consts": consts,
               "W_attn_h": w["W_attn_h"], "W_k2_h": w["W_k2_h"],
               "W_proj_h": w["W_proj_h"]}
        p.push_scope()
        emit_l0attn(p, e, io0)
        full_barrier()
        p.pop_scope()

        s0 = scr0.rearrange("(n p) c -> n p c", p=128)
        hs = None
        for tc in range(NT):
            hs = e.sq.dma(s0[tc], xres[:, tc * 1024:(tc + 1) * 1024])
        io1 = {"xres": xres, "x_ready": hs, "consts": consts, "x_hbm": scr0,
               "gate_wT_h": w["gate_wT_h"], "gate_b_h": w["gate_b_h"],
               "w13T_h": w["w13T_h"], "w2T_h": w["w2T_h"],
               "w13_sT_h": w["w13_sT_h"], "w2_sT_h": w["w2_sT_h"]}
        p.push_scope()
        emit_l0ffn(p, e, io1)
        full_barrier()
        p.pop_scope()

        s1 = scr1.rearrange("(n p) c -> n p c", p=128)
        hs = None
        for tc in range(NT):
            hs = e.sq.dma(s1[tc], xres[:, tc * 1024:(tc + 1) * 1024])
        io2 = {"xres": xres, "x_ready": hs, "consts": consts, "x_hbm": scr1,
               "W_inT_h": w["W_inT_h"], "W_xprojT_h": w["W_xprojT_h"],
               "W_dtT_h": w["W_dtT_h"], "W_outT_h": w["W_outT_h"],
               "A_log_cols_h": w["A_log_cols_h"], "D_cols_h": w["D_cols_h"],
               "bdt_cols_h": w["bdt_cols_h"], "Wconv_cols_h": w["Wconv_cols_h"]}
        p.push_scope()
        emit_l1attn(p, e, io2)
        full_barrier()
        p.pop_scope()

        io3 = {"xres": xres, "x_ready": io2["done"], "consts": consts,
               "weight_h": w["weight_h"], "bias_h": w["bias_h"]}
        p.push_scope()
        emit_l1ffn(p, e, io3)
        full_barrier()
        p.pop_scope()

        e.gq.wait(io3["done"])
        yt = y_h.rearrange("(n p) c -> n p c", p=128)
        for tc in range(NT):
            hfin = e.gq.dma(yt[tc], xres[:, tc * 1024:(tc + 1) * 1024])
        e.gq.wait(hfin)

    return p.run(body)


_PROGS = {}


def _prog_l0attn():
    if "l0a" not in _PROGS:
        _PROGS["l0a"] = standalone_block(
            emit_l0attn,
            [("W_attn_h", [1024, 2048]), ("W_k2_h", [1024, 1024]),
             ("W_proj_h", [1024, 1024])])
    return _PROGS["l0a"]


def _prog_l0ffn():
    if "l0f" not in _PROGS:
        _PROGS["l0f"] = standalone_block(
            emit_l0ffn,
            [("gate_wT_h", [1024, 16]), ("gate_b_h", [1, 16]),
             ("w13T_h", [16, 1024, 1024]), ("w2T_h", [16, 512, 1024]),
             ("w13_sT_h", [1024, 1024]), ("w2_sT_h", [512, 1024]),
             ("xT_h", [1024, 2048])])
    return _PROGS["l0f"]


def _prog_l1attn():
    if "l1a" not in _PROGS:
        _PROGS["l1a"] = standalone_block(
            emit_l1attn,
            [("W_inT_h", [1024, 4096]), ("W_xprojT_h", [2048, 96]),
             ("W_dtT_h", [64, 2048]), ("W_outT_h", [2048, 1024]),
             ("A_log_cols_h", [128, 256]), ("D_cols_h", [128, 16]),
             ("bdt_cols_h", [128, 16]), ("Wconv_cols_h", [128, 64])])
    return _PROGS["l1a"]


def _prog_l1ffn():
    if "l1f" not in _PROGS:
        _PROGS["l1f"] = standalone_block(
            emit_l1ffn,
            [("weight_h", [1024, 1024]), ("bias_h", [1, 1024])])
    return _PROGS["l1f"]


def _launch(nc, in_maps):
    res = run_bass_kernel_spmd(nc, in_maps, core_ids=list(range(8)))
    return [r["y"] for r in res.results]


def kernel(L0_attn_x, L0_attn_W_attn, L0_attn_W_k2, L0_attn_W_proj,
           L0_ffn_input_ids, L0_ffn_gate_w, L0_ffn_gate_b, L0_ffn_w13,
           L0_ffn_w2, L0_ffn_w13_s, L0_ffn_w2_s, L1_attn_W_in, L1_attn_W_conv,
           L1_attn_W_xproj, L1_attn_W_dt, L1_attn_b_dt, L1_attn_A_log,
           L1_attn_D, L1_attn_W_out, L1_ffn_weight, L1_ffn_bias):
    f32 = np.float32
    _ = np.asarray(L0_ffn_input_ids)
    x = np.ascontiguousarray(np.asarray(L0_attn_x, dtype=f32))

    Wa = np.ascontiguousarray(np.asarray(L0_attn_W_attn, f32))
    Wk2 = np.ascontiguousarray(np.asarray(L0_attn_W_k2, f32))
    Wp = np.ascontiguousarray(np.asarray(L0_attn_W_proj, f32))
    gwT = np.ascontiguousarray(np.asarray(L0_ffn_gate_w, f32).T)
    gb = np.ascontiguousarray(np.asarray(L0_ffn_gate_b, f32).reshape(1, 16))
    w13T = np.ascontiguousarray(np.asarray(L0_ffn_w13, f32).transpose(0, 2, 1))
    w2T = np.ascontiguousarray(np.asarray(L0_ffn_w2, f32).transpose(0, 2, 1))
    w13sT = np.ascontiguousarray(np.asarray(L0_ffn_w13_s, f32).T)
    w2sT = np.ascontiguousarray(np.asarray(L0_ffn_w2_s, f32).T)
    W_inT = np.ascontiguousarray(np.asarray(L1_attn_W_in, f32).T)
    W_xprojT = np.ascontiguousarray(np.asarray(L1_attn_W_xproj, f32).T)
    W_dtT = np.ascontiguousarray(np.asarray(L1_attn_W_dt, f32).T)
    W_outT = np.ascontiguousarray(np.asarray(L1_attn_W_out, f32).T)
    A_log = np.asarray(L1_attn_A_log, f32)
    A_cols = np.ascontiguousarray(
        A_log.reshape(16, 128, 16).transpose(1, 0, 2).reshape(128, 256))
    D_cols = np.ascontiguousarray(np.asarray(L1_attn_D, f32).reshape(16, 128).T)
    bdt_cols = np.ascontiguousarray(np.asarray(L1_attn_b_dt, f32).reshape(16, 128).T)
    Wconv_cols = np.ascontiguousarray(
        np.asarray(L1_attn_W_conv, f32).reshape(16, 128, 4)
        .transpose(1, 0, 2).reshape(128, 64))
    wt = np.ascontiguousarray(np.asarray(L1_ffn_weight, f32))
    bias = np.ascontiguousarray(np.asarray(L1_ffn_bias, f32).reshape(1, 1024))

    if "fused" not in _PROGS:
        _PROGS["fused"] = fused_program()
    maps = [{
        "x": x[b], "W_attn": Wa, "W_k2": Wk2, "W_proj": Wp,
        "gate_wT": gwT, "gate_b": gb, "w13T": w13T, "w2T": w2T,
        "w13_sT": w13sT, "w2_sT": w2sT,
        "W_inT": W_inT, "W_xprojT": W_xprojT, "W_dtT": W_dtT, "W_outT": W_outT,
        "A_log_cols": A_cols, "D_cols": D_cols, "bdt_cols": bdt_cols,
        "Wconv_cols": Wconv_cols, "weight": wt, "bias": bias,
    } for b in range(8)]
    ys = _launch(_PROGS["fused"], maps)
    return np.stack(ys, axis=0).astype(np.float32)
